# version 02 (final)
# baseline (speedup 1.0000x reference)
import contextlib
import numpy as np
import ml_dtypes
import concourse.bass as bass
import concourse.mybir as mybir
from concourse.bass_utils import run_bass_kernel_spmd

F32 = mybir.dt.float32
BF16 = mybir.dt.bfloat16
I32 = mybir.dt.int32
AF = mybir.ActivationFunctionType
ALU = mybir.AluOpType
AXX = mybir.AxisListType.X

EPS = 1e-5
ENGS = ("sync", "tensor", "vector", "scalar", "gpsimd")
DBG_FLAGS = set()


class P:

    def __init__(self, nc, sem_names, stack):
        self.nc = nc
        self.q = {e: [] for e in ENGS}
        self.sems = {n: stack.enter_context(nc.semaphore(n)) for n in sem_names}
        self.cnt = {n: 0 for n in sem_names}
        self.nbar = 0
        self.bankfree = {}

    def emit(self, eng, fn):
        self.q[eng].append(fn)

    DRAIN_ENGS = ("vector", "scalar", "gpsimd")

    def op(self, eng, fn, inc=None, n=1):
        if inc is None:
            self.emit(eng, fn)
        else:
            self.cnt[inc] += n
            sem = self.sems[inc]
            self.emit(eng, lambda e, fn=fn, sem=sem, n=n: fn(e).then_inc(sem, n))
        if eng in self.DRAIN_ENGS:
            self.emit(eng, lambda e: e.drain())
        return self.cnt[inc] if inc else None

    def wait(self, eng, sname, val=None):
        if val is None:
            val = self.cnt[sname]
        if val <= 0:
            return
        sem = self.sems[sname]
        self.emit(eng, lambda e, sem=sem, val=val: e.wait_ge(sem, val))

    def free(self, res, sname):
        self.bankfree[res] = (sname, self.cnt[sname])

    def acquire(self, eng, res):
        if res in self.bankfree:
            sname, v = self.bankfree[res]
            self.wait(eng, sname, v)
            del self.bankfree[res]

    def barrier(self):
        self.nbar += 1
        tgt = 5 * self.nbar
        sem = self.sems["bar"]
        for e in ENGS:
            self.emit(e, lambda en, sem=sem: en.nop().then_inc(sem, 1))
            self.emit(e, lambda en, sem=sem, tgt=tgt: en.wait_ge(sem, tgt))


def build(dbg_upto=None):
    import os
    nc = bass.Bass("TRN2", target_bir_lowering=False,
                   detect_race_conditions=bool(os.environ.get("KRACE")))
    L = {}

    L["d_xf"] = nc.dram_tensor("xf", [1024, 2051], F32, kind="ExternalInput").ap()
    L["d_win"] = nc.dram_tensor("w_in", [1024, 4368], F32, kind="ExternalInput").ap()
    L["d_wout"] = nc.dram_tensor("w_out", [2048, 1024], F32, kind="ExternalInput").ap()
    for n, shp, dt in [("conv_w", [128, 72], F32), ("conv_b", [128, 18], F32),
                       ("ln_w", [128, 8], F32), ("ln_b", [128, 8], F32),
                       ("dt_bias", [16, 1], F32), ("a_log", [16, 1], F32),
                       ("d_vec", [16, 1], F32), ("rms_w", [128, 16], F32),
                       ("f_bias", [128, 48], F32), ("f_gam", [128, 24], F32),
                       ("f_bet", [128, 24], F32), ("m_mask", [16, 4], F32),
                       ("o_hot", [16, 4], F32), ("p_mask", [128, 4], F32)]:
        L["d_" + n] = nc.dram_tensor(n, shp, dt, kind="ExternalInput").ap()
    L["d_qt"] = nc.dram_tensor("q_t", [6, 1024, 1024], I32, kind="ExternalInput").ap()
    L["d_st"] = nc.dram_tensor("s_t", [6, 1024, 1024], F32, kind="ExternalInput").ap()
    L["d_la"] = nc.dram_tensor("lora_a", [6, 32, 1024], F32, kind="ExternalInput").ap()
    L["d_lb"] = nc.dram_tensor("lora_bt", [6, 32, 1024], F32, kind="ExternalInput").ap()

    L["d_out"] = nc.dram_tensor("out", [8, 128, 2048], F32, kind="ExternalOutput").ap()
    dbgk = "ExternalOutput" if dbg_upto else "Internal"
    L["d_ysc"] = nc.dram_tensor("yscr", [16, 128, 2048], BF16, kind=dbgk).ap()
    L["d_xns"] = nc.dram_tensor("xnscr", [8, 128, 2048], F32, kind=dbgk).ap()
    agi_t = nc.dram_tensor("agin", [128, 2052], F32)
    ago_t = nc.dram_tensor("agout", [512, 2052], F32)
    L["d_agi"], L["d_ago"] = agi_t.ap(), ago_t.ap()
    L["agi_t"], L["ago_t"] = agi_t, ago_t
    if dbg_upto:
        L["d_dxnb"] = nc.dram_tensor("d_xnb", [128, 8, 2056], BF16, kind="ExternalOutput").ap()
        L["d_dxbc"] = nc.dram_tensor("d_xbc", [4, 128, 18, 520], BF16, kind="ExternalOutput").ap()
        L["d_ddt"] = nc.dram_tensor("d_dt", [16, 2048], BF16, kind="ExternalOutput").ap()
        L["d_dlag"] = nc.dram_tensor("d_lag", [16, 2048], F32, kind="ExternalOutput").ap()
        L["d_msc"] = nc.dram_tensor("d_msc", [128, 32], F32, kind="ExternalOutput").ap()
        L["d_yfm"] = nc.dram_tensor("d_yfm", [4, 128, 16, 512], BF16, kind="ExternalOutput").ap()

    consts = {
        "idb": np.eye(128, dtype=ml_dtypes.bfloat16),
        "id16": np.eye(16, dtype=np.float32),
        "idf": np.eye(128, dtype=np.float32),
        "utb": np.triu(np.ones((128, 128), np.float32)).astype(ml_dtypes.bfloat16),
        "o1f": np.ones((1, 128), np.float32),
        "o1b": np.ones((1, 128), ml_dtypes.bfloat16),
        "ocb": np.ones((128, 1), ml_dtypes.bfloat16),
        "o16": np.ones((16, 128), np.float32),
        "o128": np.ones((128, 128), np.float32),
        "id1": np.ones((1, 1), np.float32),
        "id16b": np.eye(16, dtype=ml_dtypes.bfloat16),
        "cflt": np.tile(np.array([[EPS, -1.0, 1.0, 0.0]], np.float32), (128, 1)),
    }
    eh = np.zeros((16, 2048), np.float32)
    for h in range(16):
        eh[h, h * 128:(h + 1) * 128] = 1.0
    consts["ehs"] = eh
    sel = np.zeros((128, 128), np.float32)
    sel[127, :] = 1.0
    consts["sls"] = sel
    cdr = {k: nc.inline_tensor(np.ascontiguousarray(v), "c_" + k) for k, v in consts.items()}

    sem_names = ["bar", "px", "pg", "lx", "lw0", "lw1", "lq", "lqg",
                 "ly0", "ly1", "ly2", "ly3", "lz0", "lz1", "lzl", "lo0", "lo1",
                 "lg", "lgg", "tt", "vv", "ss", "gg", "cc", "ou",
                 "ob0", "ob1", "oud"]

    with contextlib.ExitStack() as perm:
        E = perm.enter_context
        p = P(nc, sem_names, perm)
        L["p"] = p

        PF = [E(nc.psum_tensor(f"pf{i}", [128, 512], F32)) for i in range(6)]
        PB = [E(nc.psum_tensor(f"pb{i}", [128, 512], BF16)) for i in range(2)]
        L["PF"], L["PB"] = PF, PB

        for n, shp, dt in [
            ("cw", [128, 18, 4], F32), ("cb", [128, 18], F32), ("lnw", [128, 8], F32),
            ("dtb", [16, 1], F32), ("alog", [16, 1], F32), ("dv", [16, 1], F32),
            ("rmsw", [128, 16], F32), ("fbias", [128, 48], F32), ("fgam", [128, 24], F32),
            ("fbet", [128, 24], F32), ("mmask", [16, 4], F32), ("ohot", [16, 4], F32),
            ("pmask", [128, 4], F32),
            ("idb", [128, 128], BF16), ("id16", [16, 16], F32), ("idf", [128, 128], F32),
            ("utb", [128, 128], BF16), ("o1f", [1, 128], F32), ("o1b", [1, 128], BF16),
            ("ocb", [128, 1], BF16), ("o16", [16, 128], F32), ("o128", [128, 128], F32),
            ("ehs", [16, 2048], F32), ("sls", [128, 128], F32), ("id1", [1, 1], F32),
            ("o16w", [16, 512], F32), ("acol", [16, 1], F32), ("dbc", [128, 16], F32),
            ("dtbe", [16, 1], F32), ("lnbp", [128, 19], F32),
            ("pqa", [128, 2, 17], F32), ("pq2", [128, 2, 17], F32),
            ("rows", [1, 5, 520], F32), ("rowz", [1, 512], F32),
            ("scrA", [128, 512], F32), ("scrB", [128, 512], F32),
            ("scrQ", [128, 512], BF16), ("sdt", [16, 512], F32), ("id16b", [16, 16], BF16),
            ("cflt", [128, 4], F32), ("zrow", [1, 512], BF16), ("puf", [128, 128], F32),
        ]:
            L[n] = E(nc.sbuf_tensor(n, shp, dt))
        L["cdr"] = cdr

        mam = contextlib.ExitStack()
        M = mam.enter_context
        for n, shp, dt in [
            ("xnb", [128, 8, 2056], BF16), ("wsl", [128, 2, 8, 256], BF16),
            ("xbc", [128, 18, 520], BF16), ("stash", [128, 18, 4], BF16),
            ("ck", [128, 2048], BF16), ("dts", [16, 2048], BF16), ("lag", [16, 2048], F32),
            ("sdtF", [16, 512], F32),
            ("vsc", [16, 512], F32), ("H", [128, 16, 128], F32), ("Ht", [128, 16, 128], BF16),
            ("xtl", [128, 16, 128], BF16),
            ("dxh", [128, 16, 128], BF16), ("Gm", [128, 128], BF16), ("Bpp", [128, 128], BF16),
            ("Ps", [128, 2, 128], BF16), ("Ss", [128, 2, 128], BF16),
            ("laT", [128, 16], F32), ("nlaT", [128, 16], F32), ("dtT", [128, 16], BF16),
            ("lq0", [128, 16], F32), ("lq1", [128, 16], F32), ("lqz", [128, 16], F32),
            ("tmpa", [128, 16], F32), ("tmpb", [128, 16], F32), ("tmpc", [128, 16], F32),
            ("rvt", [128, 16], F32), ("avt", [128, 16], F32), ("wvt", [128, 16], F32),
            ("x4", [128, 8, 132], F32), ("xb4", [128, 8, 132], BF16),
            ("xq4", [128, 8, 132], BF16),
            ("ytl", [128, 4, 2048], BF16), ("yfm", [128, 16, 512], BF16),
            ("sinb", [128, 16, 128], BF16),
            ("tcol", [16, 4], F32), ("theta", [16, 4], F32), ("thv", [128, 16], F32),
            ("cva", [128, 512], BF16), ("cvb", [128, 512], BF16), ("cvc", [128, 512], BF16),
            ("gzt", [128, 2, 512], BF16), ("rsums", [128, 32], F32), ("qcolv", [128, 1], F32),
            ("wosb", [128, 2, 512], BF16),
            ("abct", [128, 512], F32), ("bbct", [128, 512], F32), ("dgs", [128, 128], F32),
            ("xo1", [128, 512], F32), ("xnw", [128, 2, 512], F32),
            ("xbt", [128, 512], BF16), ("xsq2", [128, 512], BF16),
        ]:
            L[n] = M(nc.sbuf_tensor(n, shp, dt))

        emit_params(nc, p, L)
        p.barrier()
        nq_run = 0 if dbg_upto == "par" else (1 if "q1" in DBG_FLAGS else 4)
        for q in range(nq_run):
            emit_a1(nc, p, L, q)
            if dbg_upto == "a1":
                continue
            emit_a2(nc, p, L, q)
            p.barrier()
            if dbg_upto == "a2":
                continue
            emit_conv(nc, p, L, q)
            p.barrier()
            if dbg_upto == "conv":
                p.op("sync", lambda e, q=q: e.dma_start(
                    out=L["d_dxbc"][q], in_=L["xbc"][:, :, :]), inc="ou", n=16)
                p.barrier()
                continue
            for kq in range(4):
                emit_ssm_chunk(nc, p, L, 4 * q + kq)
            p.barrier()
        if dbg_upto:
            p.barrier()
            p.op("sync", lambda e: e.dma_start(out=L["d_dxnb"], in_=L["xnb"][:, :, :]), inc="ou", n=16)
            p.op("sync", lambda e: e.dma_start(out=L["d_ddt"], in_=L["dts"][:, :]), inc="ou", n=16)
            p.op("sync", lambda e: e.dma_start(out=L["d_dlag"], in_=L["lag"][:, :]), inc="ou", n=16)
            p.op("vector", lambda e: e.tensor_copy(L["scrA"][:, 0:4], L["pmask"][:, :]))
            p.op("vector", lambda e: e.tensor_copy(L["scrA"][:, 4:23], L["lnbp"][:, :]), inc="vv")
            p.wait("sync", "vv")
            p.op("sync", lambda e: e.dma_start(out=L["d_msc"], in_=L["scrA"][:, 0:32]), inc="oud", n=16)
            for e_ in ENGS:
                p.wait(e_, "oud")
                p.wait(e_, "ou")
        if dbg_upto in ("par", "a1", "a2"):
            dbg_upto = "conv"
        if dbg_upto not in ("conv", "ssm"):
            emit_ag(nc, p, L)
            p.barrier()
            for kg in range(4 if "q1" in DBG_FLAGS else 16):
                emit_corr(nc, p, L, kg)
            p.barrier()
        if dbg_upto in (None, "tail", "ffn"):
            for tq in range(1 if "q1" in DBG_FLAGS else 4):
                emit_tail(nc, p, L, tq)
                p.barrier()
        mam.close()

        if dbg_upto in (None, "ffn"):
            ffn = contextlib.ExitStack()
            Fh = ffn.enter_context
            for n, shp, dt in [
                ("hbuf", [128, 8, 2048], BF16), ("y1", [128, 8, 2048], BF16),
                ("weff", [128, 2, 8, 1024], BF16), ("qsl", [128, 1024], I32),
                ("ssl", [128, 1024], BF16), ("t1s", [128, 1024], BF16),
                ("lab1", [32, 1024], BF16), ("lbb1", [32, 1024], BF16),
                ("lsc", [128, 512], BF16), ("xnt2", [128, 8, 512], F32),
                ("mbc", [128, 512], BF16), ("rbc", [128, 512], BF16),
                ("outt", [128, 2, 512], F32), ("sqs", [128, 512], BF16),
            ]:
                L[n] = Fh(nc.sbuf_tensor(n, shp, dt))
            emit_ffn(nc, p, L)
            ffn.close()

        p.wait("gpsimd", "ou")
        p.wait("gpsimd", "ob0")
        p.wait("gpsimd", "ob1")
        p.barrier()

        with nc.Block() as block:
            @block.sync
            def _(e):
                for f in p.q["sync"]:
                    f(e)

            @block.tensor
            def _(e):
                for f in p.q["tensor"]:
                    f(e)

            @block.vector
            def _(e):
                for f in p.q["vector"]:
                    f(e)

            @block.scalar
            def _(e):
                for f in p.q["scalar"]:
                    f(e)

            @block.gpsimd
            def _(e):
                for f in p.q["gpsimd"]:
                    f(e)

    return nc



def emit_params(nc, p, L):
    cdr = L["cdr"]
    pairs = [("cw", None), ("cb", "d_conv_b"), ("lnw", "d_ln_w"), ("dtb", "d_dt_bias"),
             ("alog", "d_a_log"), ("dv", "d_d_vec"), ("rmsw", "d_rms_w"),
             ("fbias", "d_f_bias"), ("fgam", "d_f_gam"), ("fbet", "d_f_bet"),
             ("mmask", "d_m_mask"), ("ohot", "d_o_hot"), ("pmask", "d_p_mask")]
    p.op("sync", lambda e: e.dma_start(
        out=L["cw"][:, :, :], in_=L["d_conv_w"].rearrange("p (a b) -> p a b", a=18)),
        inc="px", n=16)
    for dst, src in pairs[1:]:
        p.op("sync", lambda e, dst=dst, src=src: e.dma_start(
            out=L[dst][:, :], in_=L[src]), inc="px", n=16)
    for n in ["idb", "id16", "idf", "utb", "o1f", "o1b", "ocb", "o16", "o128",
              "ehs", "sls", "id1", "id16b", "cflt"]:
        p.op("sync", lambda e, n=n: e.dma_start(out=L[n][:, :], in_=cdr[n].ap()),
             inc="px", n=16)
    p.wait("vector", "px")
    p.wait("scalar", "px")
    V = "vector"
    p.op(V, lambda e: e.memset(L["xnb"][:, :, :], 0.0))
    p.op(V, lambda e: e.memset(L["xbc"][:, :, :], 0.0))
    p.op(V, lambda e: e.memset(L["dts"][:, :], 0.0))
    p.op(V, lambda e: e.memset(L["lag"][:, :], 0.0))
    p.op(V, lambda e: e.memset(L["H"][:, :, :], 0.0))
    p.op(V, lambda e: e.memset(L["Ht"][:, :, :], 0.0))
    p.op(V, lambda e: e.memset(L["lqz"][:, :], 0.0))
    p.op(V, lambda e: e.memset(L["o16w"][:, :], 1.0))
    p.op(V, lambda e: e.memset(L["rsums"][:, :], 0.0))
    p.op(V, lambda e: e.memset(L["sinb"][:, :, :], 0.0))
    p.op(V, lambda e: e.memset(L["lnbp"][:, :], 0.0))
    p.op(V, lambda e: e.tensor_copy(L["dtbe"][:, :], L["dtb"][:, :]), inc="vv")
    p.wait("gpsimd", "vv")
    p.op("gpsimd", lambda e: e.dma_start(
        out=L["xnb"][:, :, 2051:2052], in_=L["d_ln_b"].rearrange("p (g o) -> p g o", o=1),
        allow_slow_non_contiguous=True), inc="pg", n=16)
    p.wait("tensor", "pg")
    p.wait("vector", "pg")
    p.op("scalar", lambda e: e.activation(L["acol"][:, :], L["alog"][:, :], AF.Exp), inc="ss")
    p.wait(V, "ss")
    p.op(V, lambda e: e.tensor_scalar_mul(L["acol"][:, :], L["acol"][:, :], -1.0), inc="vv")
    p.op(V, lambda e: e.tensor_scalar(L["tmpa"][0:16, :], L["id16"][:, :],
                                      L["dv"][:, :], None, ALU.mult), inc="vv")
    p.wait("tensor", "vv")
    p.op("tensor", lambda e: e.matmul(L["PF"][5][0:128, 0:16], L["o16"][:, :],
                                      L["tmpa"][0:16, 0:16], start=True, stop=True), inc="tt")
    p.wait(V, "tt")
    p.op(V, lambda e: e.tensor_copy(L["dbc"][:, :], L["PF"][5][0:128, 0:16]), inc="vv")
    p.free("PF5", "vv")


def emit_a1(nc, p, L, q):
    PF = L["PF"]
    rows, rowz = L["rows"], L["rowz"]
    V, Sc, Tn = "vector", "scalar", "tensor"
    qc0 = 0 if q == 0 else 512 * q + 3
    qc1 = 512 * q + 515
    if q == 0:
        chunks = [(0, 132), (132, 260), (260, 388), (388, 516)]
    else:
        chunks = [(qc0 + 128 * i, qc0 + 128 * (i + 1)) for i in range(4)]
    xfr = L["d_xf"].rearrange("(g p) t -> p g t", g=8)
    for (c0, c1) in chunks:
        n = c1 - c0
        r0 = c0 - qc0
        rS = lambda i, r0=r0, n=n: rows[0:1, i, r0:r0 + n]
        rZ = lambda n=n: rowz[0:1, 0:n]
        p.acquire("sync", "x4")
        p.op("sync", lambda e, c0=c0, c1=c1, n=n: e.dma_start(
            out=L["x4"][:, :, 0:n], in_=xfr[:, :, c0:c1]), inc="lx", n=16)
        p.wait(V, "lx")
        p.acquire(V, "xb4")
        p.op(V, lambda e, n=n: e.tensor_copy(L["xb4"][:, :, 0:n], L["x4"][:, :, 0:n]), inc="vv")
        p.wait(Sc, "vv")
        p.acquire(Sc, "xq4")
        p.op(Sc, lambda e, n=n: e.activation(
            L["xq4"][:, :, 0:n], L["xb4"][:, :, 0:n], AF.Square), inc="ss")
        p.acquire(Tn, "PF0")
        p.acquire(Tn, "PF1")
        p.wait(Tn, "vv")
        p.wait(Tn, "ss")
        for g in range(8):
            p.op(Tn, lambda e, g=g, n=n: e.matmul(
                PF[0][0:1, 0:n], L["ocb"][:, :], L["xb4"][:, g, 0:n],
                start=(g == 0), stop=(g == 7)))
        for g in range(8):
            p.op(Tn, lambda e, g=g, n=n: e.matmul(
                PF[1][0:1, 0:n], L["ocb"][:, :], L["xq4"][:, g, 0:n],
                start=(g == 0), stop=(g == 7)), inc="tt" if g == 7 else None)
        p.free("xb4", "tt")
        p.free("xq4", "tt")
        p.wait(V, "tt")
        p.op(V, lambda e, n=n, rS=rS: e.tensor_scalar(rS(2), PF[0][0:1, 0:n], 1.0 / 1024, None, ALU.mult))
        p.op(V, lambda e, n=n, rS=rS: e.tensor_scalar(rS(3), PF[1][0:1, 0:n], 1.0 / 1024, None, ALU.mult),
             inc="vv")
        p.free("PF0", "vv")
        p.free("PF1", "vv")
        p.op(V, lambda e, rS=rS, rZ=rZ: e.tensor_mul(rZ(), rS(2), rS(2)))
        p.op(V, lambda e, rS=rS, rZ=rZ: e.tensor_sub(rS(3), rS(3), rZ()), inc="vv")
        p.wait(Sc, "vv")
        p.op(Sc, lambda e, rS=rS, rZ=rZ: e.activation(rZ(), rS(3), AF.Ln, bias=L["cflt"][0:1, 0:1]))
        p.op(Sc, lambda e, rZ=rZ: e.activation(rZ(), rZ(), AF.Exp, scale=-0.5), inc="ss")
        p.wait(V, "ss")
        p.op(V, lambda e, rS=rS, rZ=rZ: e.tensor_scalar_add(rS(0), rZ(), 1.0))
        p.op(V, lambda e, rS=rS, rZ=rZ: e.tensor_mul(rS(1), rS(2), rZ()))
        p.op(V, lambda e, rS=rS: e.tensor_scalar_add(rS(4), rS(3), EPS))
        p.op(V, lambda e, rS=rS: e.reciprocal(rS(4), rS(4)))
        p.op(V, lambda e, rS=rS: e.tensor_mul(rS(4), rS(3), rS(4)), inc="vv")
        p.wait(Sc, "vv")
        p.op(Sc, lambda e, rS=rS: e.activation(rS(4), rS(4), AF.Ln, bias=L["cflt"][0:1, 0:1]))
        p.op(Sc, lambda e, rS=rS: e.activation(rS(4), rS(4), AF.Exp, scale=-0.5), inc="ss")
        p.wait(V, "ss")
        p.op(V, lambda e, rS=rS, rZ=rZ: e.tensor_mul(rS(4), rS(4), rZ()))
        p.op(V, lambda e, rS=rS: e.tensor_mul(rS(3), rS(2), rS(4)), inc="vv")
        p.acquire(Tn, "PF2")
        p.acquire(Tn, "PF3")
        p.wait(Tn, "vv")
        p.op(Tn, lambda e, n=n, rS=rS: e.matmul(PF[2][:, 0:n], L["o1f"][:, :], rS(4),
                                         start=True, stop=True))
        p.op(Tn, lambda e, n=n, rS=rS: e.matmul(PF[3][:, 0:n], L["o1f"][:, :], rS(3),
                                         start=True, stop=True), inc="tt")
        p.wait(V, "tt")
        for g in range(8):
            p.op(V, lambda e, g=g, n=n: e.tensor_mul(
                L["scrA"][:, 0:n], L["x4"][:, g, 0:n], PF[2][:, 0:n]))
            p.op(V, lambda e, g=g, n=n, c0=c0: e.tensor_sub(
                L["xnb"][:, g, c0:c0 + n], L["scrA"][:, 0:n], PF[3][:, 0:n]),
                inc="vv" if g == 7 else None)
        p.free("PF2", "vv")
        p.free("PF3", "vv")
        p.free("x4", "vv")
    for g in range(8):
        p.op(V, lambda e, g=g, qc0=qc0, qc1=qc1: e.tensor_scalar_mul(
            L["xnb"][:, g, qc0:qc1], L["xnb"][:, g, qc0:qc1], L["lnw"][:, g:g + 1]),
            inc="vv" if g == 7 else None)
    p.acquire(Tn, "PF4")
    for sl_, dc in ((0, 0), (1, 1)):
        for i in range(4):
            roff = 512 * q + 128 * i + 3 - qc0
            p.op(Tn, lambda e, sl_=sl_, i=i, roff=roff: e.transpose(
                PF[4][:, 4 * sl_ + i:4 * sl_ + i + 1],
                rows[0:1, sl_, roff:roff + 128], L["id1"][:, :]),
                inc="tt" if (sl_, i) == (1, 3) else None)
    p.wait(V, "tt")
    p.op(V, lambda e, q=q: e.tensor_copy(L["pqa"][:, 0, 4 * q:4 * q + 4], PF[4][:, 0:4]))
    p.op(V, lambda e, q=q: e.tensor_copy(L["pqa"][:, 1, 4 * q:4 * q + 4], PF[4][:, 4:8]),
         inc="vv")
    p.free("PF4", "vv")


def emit_a2(nc, p, L, q):
    PF = L["PF"]
    xnb, wsl, xbc = L["xnb"], L["wsl"], L["xbc"]
    V, Sc, Tn = "vector", "scalar", "tensor"
    c0m = 512 * q + 3
    p.wait(Tn, "vv")
    groups = [(i, 2048 + 256 * i, 256, [2 * i, 2 * i + 1]) for i in range(9)]
    if "nodt" not in DBG_FLAGS:
        groups.append((9, 4352, 16, [18]))
    for gi, w0, ncol, ofs in groups:
        buf = gi % 2
        p.acquire("gpsimd", f"wsl{buf}")
        for k in range(8):
            p.op("gpsimd", lambda e, k=k, w0=w0, ncol=ncol, buf=buf: e.dma_start(
                out=wsl[:, buf, k, 0:ncol],
                in_=L["d_win"][128 * k:128 * (k + 1), w0:w0 + ncol]), inc=f"lw{buf}", n=16)
        p.wait(Tn, f"lw{buf}")
        if "nomm" in DBG_FLAGS:
            continue
        for ofl, of in enumerate(ofs):
            isdt = (of == 18)
            wc0 = 128 * ofl if not isdt else 0
            wcn = 128 if not isdt else 16
            pm = PF[ofl % 2]
            p.acquire(Tn, f"PF{ofl % 2}")
            orow = 16 if isdt else 128
            for k in range(8):
                p.op(Tn, lambda e, k=k, buf=buf, wc0=wc0, wcn=wcn, pm=pm, c0m=c0m, orow=orow:
                     e.matmul(pm[0:orow, 0:512], wsl[:, buf, k, wc0:wc0 + wcn],
                              xnb[:, k, c0m:c0m + 512], start=(k == 0), stop=(k == 7)),
                     inc="tt" if k == 7 else None)
            if q == 0 and "noprefix" not in DBG_FLAGS:
                pp = PF[2 + ofl % 2]
                p.acquire(Tn, f"PF{2 + ofl % 2}")
                for k in range(8):
                    p.op(Tn, lambda e, k=k, buf=buf, wc0=wc0, wcn=wcn, pp=pp, orow=orow:
                         e.matmul(pp[0:orow, 0:128], wsl[:, buf, k, wc0:wc0 + wcn],
                                  xnb[:, k, 0:128], start=(k == 0), stop=False))
                for k in range(8):
                    p.op(Tn, lambda e, k=k, buf=buf, wc0=wc0, wcn=wcn, pp=pp, orow=orow:
                         e.matmul(pp[0:orow, 128:256], wsl[:, buf, k, wc0:wc0 + wcn],
                                  xnb[:, k, 1924:2052], start=False, stop=(k == 7)),
                         inc="tt" if k == 7 else None)
            p.wait(V, "tt")
            if "noevac" in DBG_FLAGS:
                p.op(V, lambda e, pm=pm: e.tensor_copy(L["scrA"][0:1, 0:4], pm[0:1, 0:4]),
                     inc="vv")
                p.free(f"PF{ofl % 2}", "vv")
                if q == 0 and "noprefix" not in DBG_FLAGS:
                    p.free(f"PF{2 + ofl % 2}", "vv")
                continue
            if not isdt:
                if q == 0 and "noprefix" not in DBG_FLAGS:
                    pp = PF[2 + ofl % 2]
                    p.op(V, lambda e, of=of, pp=pp: e.tensor_copy(
                        L["lnbp"][:, of:of + 1], pp[0:128, 255:256]))
                    p.op(V, lambda e, of=of, pp=pp: e.tensor_scalar_add(
                        xbc[:, of, 0:3], pp[0:128, 0:3], L["lnbp"][:, of:of + 1]))
                    p.op(V, lambda e, of=of: e.tensor_mul(
                        xbc[:, of, 0:3], xbc[:, of, 0:3], L["pmask"][:, 0:3]), inc="vv")
                    p.free(f"PF{2 + ofl % 2}", "vv")
                if "evacsc" in DBG_FLAGS:
                    p.wait(Sc, "tt")
                    p.op(Sc, lambda e, of=of, pm=pm: e.activation(
                        xbc[:, of, 3:515], pm[0:128, 0:512], AF.Identity,
                        bias=L["lnbp"][:, of:of + 1]), inc="ss")
                    p.free(f"PF{ofl % 2}", "ss")
                    p.op(V, lambda e: e.nop(), inc="vv")
                elif "evacscr" in DBG_FLAGS:
                    p.op(V, lambda e, of=of, pm=pm: e.tensor_scalar_add(
                        L["scrQ"][:, 0:512], pm[0:128, 0:512], L["lnbp"][:, of:of + 1]), inc="vv")
                    p.free(f"PF{ofl % 2}", "vv")
                else:
                    p.op(V, lambda e, of=of, pm=pm: e.tensor_scalar_add(
                        xbc[:, of, 3:515], pm[0:128, 0:512], L["lnbp"][:, of:of + 1]), inc="vv")
                    p.free(f"PF{ofl % 2}", "vv")
            else:
                if q == 0 and "noprefix" not in DBG_FLAGS:
                    pp = PF[2 + ofl % 2]
                    p.op(V, lambda e, pp=pp: e.tensor_copy(
                        L["lnbp"][0:16, 18:19], pp[0:16, 255:256]))
                    p.op(V, lambda e: e.tensor_add(
                        L["dtbe"][:, :], L["dtb"][:, :], L["lnbp"][0:16, 18:19]), inc="vv")
                    p.free(f"PF{2 + ofl % 2}", "vv")
                    p.wait(Sc, "vv")
                p.wait(Sc, "tt")
                p.op(Sc, lambda e, pm=pm: e.activation(
                    L["sdt"][:, :], pm[0:16, 0:512], AF.Exp, bias=L["dtbe"][:, :]), inc="ss")
                p.wait(V, "ss")
                p.op(V, lambda e: e.tensor_scalar_add(L["sdt"][:, :], L["sdt"][:, :], 1.0),
                     inc="vv")
                p.free(f"PF{ofl % 2}", "ss")
                p.wait(Sc, "vv")
                p.op(Sc, lambda e: e.activation(
                    L["sdtF"][:, :], L["sdt"][:, :], AF.Ln), inc="ss")
                p.wait(V, "ss")
                p.op(V, lambda e, q=q: e.tensor_copy(
                    L["dts"][:, 512 * q:512 * q + 512], L["sdtF"][:, :]), inc="vv")
        p.free(f"wsl{buf}", "tt")
    p.wait(V, "ss")
    p.op(V, lambda e, q=q: e.tensor_scalar(
        L["vsc"][:, :], L["sdtF"][:, :], L["acol"][:, :], None, ALU.mult))
    if q == 0:
        p.op(V, lambda e: e.tensor_tensor_scan(
            L["lag"][:, 0:512], L["o16w"][:, :], L["vsc"][:, :], 0.0,
            ALU.mult, ALU.add), inc="vv")
    else:
        p.op(V, lambda e, q=q: e.tensor_tensor_scan(
            L["lag"][:, 512 * q:512 * q + 512], L["o16w"][:, :], L["vsc"][:, :],
            L["lag"][:, 512 * q - 1:512 * q], ALU.mult, ALU.add), inc="vv")


def emit_conv(nc, p, L, q):
    xbc, cw, cb = L["xbc"], L["cw"], L["cb"]
    V, G, Sc = "vector", "gpsimd", "scalar"
    if q > 0:
        p.op(V, lambda e: e.tensor_copy(xbc[:, :, 0:3], L["stash"][:, :, 0:3]), inc="vv")
    p.op(V, lambda e: e.tensor_copy(L["stash"][:, :, 0:3], xbc[:, :, 512:515]), inc="vv")
    p.wait(G, "vv")
    for ct in range(18):
        eng = V if ct % 3 else G
        if eng == V:
            ta, tb, tc = L["cva"][:, :], L["cvb"][:, :], L["cvc"][:, :]
            isem = "vv"
        else:
            ta, tb, tc = L["gzt"][:, 0, :], L["gzt"][:, 1, :], L["scrQ"][:, :]
            isem = "gg"
        p.acquire(eng, f"cvsc_{isem}")
        w0, w1 = cw[:, ct, 0:1], cw[:, ct, 1:2]
        w2, w3 = cw[:, ct, 2:3], cw[:, ct, 3:4]
        p.op(eng, lambda e, ct=ct, ta=ta, w0=w0: e.tensor_scalar(
            ta, xbc[:, ct, 0:512], w0, None, ALU.mult))
        p.op(eng, lambda e, ct=ct, tb=tb, w1=w1: e.tensor_scalar(
            tb, xbc[:, ct, 1:513], w1, None, ALU.mult))
        p.op(eng, lambda e, ta=ta, tb=tb: e.tensor_add(ta, ta, tb))
        p.op(eng, lambda e, ct=ct, tb=tb, w2=w2: e.tensor_scalar(
            tb, xbc[:, ct, 2:514], w2, None, ALU.mult))
        p.op(eng, lambda e, ct=ct, tc=tc, w3=w3: e.tensor_scalar(
            tc, xbc[:, ct, 3:515], w3, None, ALU.mult))
        p.op(eng, lambda e, tb=tb, tc=tc: e.tensor_add(tb, tb, tc))
        p.op(eng, lambda e, ta=ta, tb=tb: e.tensor_add(ta, ta, tb), inc=isem)
        p.wait(Sc, isem)
        p.op(Sc, lambda e, ct=ct, ta=ta: e.activation(
            xbc[:, ct, 3:515], ta, AF.Silu, bias=cb[:, ct:ct + 1]), inc="ss")
        p.free(f"cvsc_{isem}", "ss")
    p.wait(V, "ss")
    p.wait(G, "ss")


def emit_ssm_chunk(nc, p, L, kg):
    PF, PB = L["PF"], L["PB"]
    xbc, lag, dts = L["xbc"], L["lag"], L["dts"]
    V, Sc, Tn = "vector", "scalar", "tensor"
    t0 = 128 * kg
    cc = slice(3 + 128 * (kg % 4), 3 + 128 * (kg % 4) + 128)
    lqp = L["lqz"] if kg == 0 else (L["lq0"] if (kg - 1) % 2 == 0 else L["lq1"])
    lqc = L["lq0"] if kg % 2 == 0 else L["lq1"]
    p.acquire(Tn, "PF4")
    p.acquire(Tn, "PB0")
    p.op(Tn, lambda e: e.transpose(PF[4][:, 0:16], lag[:, t0:t0 + 128], L["id16"][:, :]))
    p.op(Tn, lambda e: e.transpose(PB[0][:, 0:16], dts[:, t0:t0 + 128], L["id16b"][:, :]),
         inc="tt")
    p.wait(V, "tt")
    p.acquire(V, "laT")
    p.op(V, lambda e: e.tensor_copy(L["laT"][:, :], PF[4][:, 0:16]))
    p.op(V, lambda e: e.tensor_copy(L["dtT"][:, :], PB[0][:, 0:16]))
    p.op(V, lambda e: e.tensor_scalar_mul(L["nlaT"][:, :], L["laT"][:, :], -1.0), inc="vv")
    p.free("PF4", "vv")
    p.free("PB0", "vv")
    p.acquire(Tn, "PF5")
    p.wait(Tn, "vv")
    p.op(Tn, lambda e: e.matmul(PF[5][0:128, 0:16], L["sls"][:, :], L["laT"][:, :],
                                start=True, stop=True), inc="tt")
    p.wait(V, "tt")
    p.op(V, lambda e, lqc=lqc: e.tensor_copy(lqc[:, :], PF[5][0:128, 0:16]), inc="vv")
    p.free("PF5", "vv")
    p.op(V, lambda e, lqp=lqp: e.tensor_sub(L["tmpa"][:, :], L["laT"][:, :], lqp[:, :]))
    p.op(V, lambda e, lqc=lqc, lqp=lqp: e.tensor_sub(L["tmpb"][:, :], lqc[:, :], lqp[:, :]))
    p.op(V, lambda e, lqc=lqc: e.tensor_sub(L["tmpc"][:, :], lqc[:, :], L["laT"][:, :]),
         inc="vv")
    p.wait(Sc, "vv")
    p.acquire(Sc, "rawv")
    p.op(Sc, lambda e: e.activation(L["rvt"][:, :], L["tmpa"][:, :], AF.Exp))
    p.op(Sc, lambda e: e.activation(L["avt"][:, :], L["tmpb"][:, :], AF.Exp))
    p.op(Sc, lambda e: e.activation(L["wvt"][:, :], L["tmpc"][:, :], AF.Exp), inc="ss")
    p.acquire(Tn, "PB0")
    p.acquire(Tn, "PF0")
    p.op(Tn, lambda e: e.transpose(PB[0][:, 0:128], xbc[:, 16, cc], L["idb"][:, :]))
    p.op(Tn, lambda e: e.matmul(PF[0][:, 0:128], xbc[:, 16, cc], xbc[:, 17, cc],
                                start=True, stop=True), inc="tt")
    p.wait(V, "tt")
    p.acquire(V, "Bpp")
    p.op(V, lambda e: e.tensor_copy(L["Bpp"][:, :], PB[0][:, 0:128]))
    p.op(V, lambda e: e.tensor_tensor(L["Gm"][:, :], PF[0][:, 0:128], L["utb"][:, :],
                                      ALU.mult))
    p.op(V, lambda e: e.tensor_copy(L["ck"][:, t0:t0 + 128], xbc[:, 17, cc]), inc="vv")
    p.free("PB0", "vv")
    p.free("PF0", "vv")
    for g4 in range(4):
        bk = g4 % 2
        p.acquire(Tn, f"PB{bk}")
        for i in range(4):
            p.op(Tn, lambda e, g4=g4, i=i, bk=bk: e.transpose(
                PB[bk][:, 128 * i:128 * i + 128], xbc[:, 4 * g4 + i, cc], L["idb"][:, :]),
                inc="tt" if i == 3 else None)
        p.wait(V, "tt")
        dt_ap = bass.AP(L["dtT"], 4 * g4, [[16, 128], [1, 4], [0, 128]])
        db_ap = bass.AP(L["dbc"], 4 * g4, [[16, 128], [1, 4], [0, 128]])
        p.op(V, lambda e, g4=g4, bk=bk, dt_ap=dt_ap: e.tensor_tensor(
            L["xtl"][:, 4 * g4:4 * g4 + 4, :],
            PB[bk][:, 0:512].rearrange("p (h t) -> p h t", h=4), dt_ap, ALU.mult))
        p.op(V, lambda e, g4=g4, bk=bk, db_ap=db_ap: e.tensor_tensor(
            L["dxh"][:, 4 * g4:4 * g4 + 4, :],
            PB[bk][:, 0:512].rearrange("p (h t) -> p h t", h=4), db_ap, ALU.mult),
            inc="vv")
        p.free(f"PB{bk}", "vv")

    for g4 in range(4):
        ybi = 3 if g4 % 2 == 0 else 5
        ybe = 4 if g4 % 2 == 0 else 0
        p.acquire(Tn, f"PF{ybi}")
        p.acquire(Tn, f"PF{ybe}")
        for hh in range(4):
            h = 4 * g4 + hh
            pbk = 1 if h % 2 else 2
            p.acquire(Tn, f"PF{pbk}")
            p.wait(Tn, "ss")
            p.op(Tn, lambda e, h=h, pbk=pbk: e.matmul(
                PF[pbk][:, 0:128], L["ehs"][:, 128 * h:128 * h + 128],
                lag[0:16, t0:t0 + 128], start=True, stop=True), inc="tt")
            p.acquire(Sc, f"Ps{h % 2}")
            p.wait(Sc, "tt")
            p.op(Sc, lambda e, h=h, pbk=pbk: e.activation(
                L["puf"][:, :], PF[pbk][:, 0:128], AF.Relu,
                bias=L["laT"][:, h:h + 1], scale=-1.0))
            p.op(Sc, lambda e, h=h: e.activation(
                L["Ps"][:, h % 2, :], L["puf"][:, :], AF.Exp, scale=-1.0), inc="ss")
            p.free(f"PF{pbk}", "ss")
            p.acquire(V, f"Ss{h % 2}")
            p.wait(V, "ss")
            p.op(V, lambda e, h=h: e.tensor_tensor(
                L["Ss"][:, h % 2, :], L["Ps"][:, h % 2, :], L["Gm"][:, :],
                ALU.mult), inc="vv")
            p.free(f"Ps{h % 2}", "vv")
            p.wait(Tn, "vv")
            p.op(Tn, lambda e, h=h, hh=hh, ybi=ybi: e.matmul(
                PF[ybi][:, 128 * hh:128 * hh + 128], L["Ss"][:, h % 2, :],
                L["xtl"][:, h, :], start=(hh == 0), stop=(hh == 3)), inc="tt")
            p.free(f"Ss{h % 2}", "tt")
            p.op(Tn, lambda e, h=h, hh=hh, ybe=ybe: e.matmul(
                PF[ybe][:, 128 * hh:128 * hh + 128], xbc[:, 17, cc],
                L["Ht"][:, h, :], start=(hh == 0), stop=(hh == 3)),
                inc="tt" if hh == 3 else None)
        p.wait(V, "tt")
        rv_ap = bass.AP(L["rvt"], 4 * g4, [[16, 128], [1, 4], [0, 128]])
        slot = kg % 2
        p.acquire(V, f"ytl{slot}") if g4 == 0 else None
        p.op(V, lambda e, ybe=ybe, rv_ap=rv_ap: e.tensor_tensor(
            L["scrA"][:, 0:512].rearrange("p (h t) -> p h t", h=4),
            PF[ybe][:, 0:512].rearrange("p (h t) -> p h t", h=4), rv_ap, ALU.mult))
        p.op(V, lambda e, ybi=ybi: e.tensor_add(
            L["scrB"][:, 0:512], PF[ybi][:, 0:512], L["scrA"][:, 0:512]))
        p.op(V, lambda e, g4=g4, slot=slot: e.tensor_tensor(
            L["ytl"][:, slot, 512 * g4:512 * g4 + 512], L["scrB"][:, 0:512],
            L["dxh"][:, 4 * g4:4 * g4 + 4, :].rearrange("p h t -> p (h t)"), ALU.add),
            inc="vv")
        p.free(f"PF{ybi}", "vv")
        p.free(f"PF{ybe}", "vv")
        if g4 == 3:
            p.free("laT", "ss")
    for g4 in range(4):
        bk = 1 if g4 % 2 else 2
        xsc = L["cva"] if g4 % 2 == 0 else L["cvb"]
        wv_ap = bass.AP(L["wvt"], 4 * g4, [[16, 128], [1, 4], [0, 128]])
        p.acquire(V, f"xsc{g4 % 2}")
        p.op(V, lambda e, g4=g4, xsc=xsc, wv_ap=wv_ap: e.tensor_tensor(
            xsc[:, 0:512].rearrange("p (h t) -> p h t", h=4),
            L["xtl"][:, 4 * g4:4 * g4 + 4, :], wv_ap, ALU.mult), inc="vv")
        p.acquire(Tn, f"PF{bk}")
        p.wait(Tn, "vv")
        for i in range(4):
            p.op(Tn, lambda e, g4=g4, i=i, bk=bk, xsc=xsc: e.matmul(
                PF[bk][:, 128 * i:128 * i + 128], L["Bpp"][:, :],
                xsc[:, 128 * i:128 * i + 128], start=(i == 0), stop=(i == 3)),
                inc="tt" if i == 3 else None)
        p.free(f"xsc{g4 % 2}", "tt")
        if g4 == 3:
            p.free("Bpp", "tt")
        p.wait(V, "tt")
        av_ap = bass.AP(L["avt"], 4 * g4, [[16, 128], [1, 4], [0, 128]])
        p.acquire(V, f"Hg{g4}")
        p.op(V, lambda e, g4=g4, av_ap=av_ap: e.tensor_tensor(
            L["scrA"][:, 0:512].rearrange("p (h t) -> p h t", h=4),
            L["H"][:, 4 * g4:4 * g4 + 4, :], av_ap, ALU.mult))
        p.op(V, lambda e, g4=g4, bk=bk: e.tensor_add(
            L["H"][:, 4 * g4:4 * g4 + 4, :].rearrange("p h t -> p (h t)"),
            L["scrA"][:, 0:512], PF[bk][:, 0:512]), inc="vv")
        p.free(f"PF{bk}", "vv")
        p.wait(Sc, "vv")
        p.op(Sc, lambda e, g4=g4: e.copy(
            L["Ht"][:, 4 * g4:4 * g4 + 4, :], L["H"][:, 4 * g4:4 * g4 + 4, :]), inc="ss")
        p.free(f"Hg{g4}", "ss")
        if g4 == 3:
            p.free("rawv", "vv")
    slot = kg % 2
    p.wait("sync", "vv")
    p.op("sync", lambda e, kg=kg, slot=slot: e.dma_start(
        out=L["d_ysc"][kg], in_=L["ytl"][:, slot, :]), inc=f"ly{slot}", n=16)
    p.free(f"ytl{slot}", f"ly{slot}")


def emit_ag(nc, p, L):
    V, Sc, Tn, G = "vector", "scalar", "tensor", "gpsimd"
    p.op("sync", lambda e: e.dma_start(
        out=L["d_agi"][:, 0:2048],
        in_=L["H"][:, :, :].rearrange("p h t -> p (h t)")), inc="lg", n=16)
    p.op("sync", lambda e: e.dma_start(
        out=L["d_agi"][:, 2048:2052], in_=L["lqz"][:, 0:4],
        allow_slow_non_contiguous=True), inc="lg", n=16)
    p.wait("sync", "lg")
    p.op("sync", lambda e: e.dma_start(
        out=L["d_agi"][0:16, 2048:2049], in_=L["lag"][:, 2047:2048],
        allow_slow_non_contiguous=True), inc="lg", n=16)
    p.wait(G, "lg")
    p.op(G, lambda e: e.collective_compute(
        "AllGather", ALU.bypass, replica_groups=[[0, 1, 2, 3], [4, 5, 6, 7]],
        ins=[L["agi_t"].ap().opt()], outs=[L["ago_t"].ap().opt()]), inc="cc")
    p.wait("sync", "cc")
    p.wait(G, "cc")
    for j in range(4):
        p.op("sync", lambda e, j=j: e.dma_start(
            out=L["tcol"][:, j:j + 1], in_=L["d_ago"][128 * j:128 * j + 16, 2048:2049],
            allow_slow_non_contiguous=True), inc="lg", n=16)
    p.wait(V, "lg")
    p.op(V, lambda e: e.tensor_tensor_scan(
        L["theta"][:, :], L["o16w"][:, 0:4], L["tcol"][:, :], 0.0, ALU.mult, ALU.add))
    p.op(V, lambda e: e.tensor_mul(L["tcol"][:, :], L["theta"][:, :], L["ohot"][:, :]))
    p.op(V, lambda e: e.reduce_sum(L["dtbe"][:, 0:1] if False else L["tmpa"][0:16, 0:1],
                                   L["tcol"][:, :], AXX))
    p.op(V, lambda e: e.tensor_scalar(
        L["theta"][:, :], L["theta"][:, :], L["tmpa"][0:16, 0:1], 0.0,
        ALU.subtract, ALU.max), inc="vv")
    p.wait(Sc, "vv")
    p.op(Sc, lambda e: e.activation(L["theta"][:, :], L["theta"][:, :], AF.Exp, scale=-1.0),
         inc="ss")
    p.wait(V, "ss")
    p.op(V, lambda e: e.tensor_mul(L["theta"][:, :], L["theta"][:, :], L["mmask"][:, :]),
         inc="vv")
    for j in range(4):
        p.acquire(G, "gtile")
        p.op(G, lambda e, j=j: e.dma_start(
            out=L["ytl"][:, 2, :], in_=L["d_ago"][128 * j:128 * j + 128, 0:2048]),
            inc="lgg", n=16)
        p.op(V, lambda e, j=j: e.tensor_scalar(
            L["tmpb"][0:16, :], L["id16"][:, :], L["theta"][:, j:j + 1], None, ALU.mult),
            inc="vv")
        p.acquire(Tn, "PF5")
        p.wait(Tn, "vv")
        p.op(Tn, lambda e: e.matmul(L["PF"][5][0:128, 0:16], L["o16"][:, :],
                                    L["tmpb"][0:16, 0:16], start=True, stop=True), inc="tt")
        p.wait(V, "tt")
        p.op(V, lambda e: e.tensor_copy(L["thv"][:, :], L["PF"][5][0:128, 0:16]), inc="vv")
        p.free("PF5", "vv")
        p.wait(V, "lg")
        p.wait(V, "lgg")
        th_ap = bass.AP(L["thv"], 0, [[16, 128], [1, 16], [0, 128]])
        p.op(V, lambda e, th_ap=th_ap: e.tensor_tensor(
            L["xtl"][:, :, :], L["ytl"][:, 2, :].rearrange("p (h t) -> p h t", h=16),
            th_ap, ALU.mult), inc="vv")
        p.free("gtile", "vv")
        p.op(V, lambda e: e.tensor_add(
            L["sinb"][:, :, :].rearrange("p h t -> p (h t)"),
            L["sinb"][:, :, :].rearrange("p h t -> p (h t)"),
            L["xtl"][:, :, :].rearrange("p h t -> p (h t)")), inc="vv")


def emit_corr(nc, p, L, kg):
    PF = L["PF"]
    V, Sc, Tn = "vector", "scalar", "tensor"
    t0 = 128 * kg
    slot = 2 + (kg % 2)
    p.acquire("sync", f"ytl{slot}")
    p.wait("sync", f"ly{kg % 2}")
    p.op("sync", lambda e, kg=kg, slot=slot: e.dma_start(
        out=L["ytl"][:, slot, :], in_=L["d_ysc"][kg]), inc=f"ly{slot}", n=16)
    p.acquire(Tn, "PF4")
    p.op(Tn, lambda e: e.transpose(PF[4][:, 0:16], L["lag"][:, t0:t0 + 128],
                                   L["id16"][:, :]), inc="tt")
    p.wait(V, "tt")
    p.op(V, lambda e: e.tensor_copy(L["laT"][:, :], PF[4][:, 0:16]), inc="vv")
    p.free("PF4", "vv")
    p.wait(Sc, "vv")
    p.op(Sc, lambda e: e.activation(L["rvt"][:, :], L["laT"][:, :], AF.Exp), inc="ss")
    for g4 in range(4):
        p.acquire(Tn, f"PF{g4}")
        p.op(Tn, lambda e, g4=g4: e.matmul(
            PF[g4][:, 0:512], L["ck"][:, t0:t0 + 128],
            L["sinb"][:, 4 * g4:4 * g4 + 4, :].rearrange("p h t -> p (h t)"),
            start=True, stop=True), inc="tt")
    p.wait(V, "tt")
    p.wait(V, "ss")
    p.wait(V, f"ly{slot}")
    for g4 in range(4):
        rv_ap = bass.AP(L["rvt"], 4 * g4, [[16, 128], [1, 4], [0, 128]])
        p.op(V, lambda e, g4=g4, rv_ap=rv_ap: e.tensor_tensor(
            L["scrA"][:, 0:512].rearrange("p (h t) -> p h t", h=4),
            PF[g4][:, 0:512].rearrange("p (h t) -> p h t", h=4), rv_ap, ALU.mult))
        p.op(V, lambda e, g4=g4, slot=slot: e.tensor_add(
            L["ytl"][:, slot, 512 * g4:512 * g4 + 512],
            L["ytl"][:, slot, 512 * g4:512 * g4 + 512], L["scrA"][:, 0:512]),
            inc="vv")
        p.free(f"PF{g4}", "vv")
    p.wait("sync", "vv")
    p.op("sync", lambda e, kg=kg, slot=slot: e.dma_start(
        out=L["d_ysc"][kg], in_=L["ytl"][:, slot, :]), inc=f"ly{slot}", n=16)
    p.free(f"ytl{slot}", f"ly{slot}")


def emit_tail(nc, p, L, tq):
    PF, PB = L["PF"], L["PB"]
    xnb, wsl = L["xnb"], L["wsl"]
    V, Sc, Tn, G = "vector", "scalar", "tensor", "gpsimd"
    p.wait("sync", "ly2")
    p.wait("sync", "ly3")
    for i in range(4):
        p.acquire("sync", f"ytl{i}")
        p.op("sync", lambda e, i=i, tq=tq: e.dma_start(
            out=L["ytl"][:, i, :], in_=L["d_ysc"][4 * tq + i]), inc=f"ly{i}", n=16)
        p.wait(V, f"ly{i}")
    for ofc in range(8):
        buf = ofc % 2
        p.acquire(G, f"wsl{buf}")
        for k in range(8):
            p.op(G, lambda e, k=k, ofc=ofc, buf=buf: e.dma_start(
                out=wsl[:, buf, k, 0:256],
                in_=L["d_win"][128 * k:128 * (k + 1), 256 * ofc:256 * ofc + 256]),
                inc=f"lw{buf}", n=16)
        p.wait(Tn, f"lw{buf}")
        p.acquire(Tn, "PF4")
        for k in range(8):
            p.op(Tn, lambda e, k=k, buf=buf: e.matmul(
                PF[4][0:1, 0:256], xnb[:, k, 2051:2052], wsl[:, buf, k, 0:256],
                start=(k == 0), stop=(k == 7)), inc="tt" if k == 7 else None)
        p.wait(V, "tt")
        p.acquire(V, "zrow")
        p.op(V, lambda e: e.tensor_copy(L["zrow"][0:1, 0:256], PF[4][0:1, 0:256]), inc="vv")
        p.free("PF4", "vv")
        p.wait(Tn, "vv")
        for ti in range(4):
            bk = ti % 2
            tc0 = 512 * tq + 128 * ti + 3
            p.acquire(Tn, f"PF{bk}")
            for k in range(8):
                p.op(Tn, lambda e, k=k, buf=buf, bk=bk, tc0=tc0: e.matmul(
                    PF[bk][:, 0:256], xnb[:, k, tc0:tc0 + 128], wsl[:, buf, k, 0:256],
                    start=(k == 0), stop=False))
            p.op(Tn, lambda e, bk=bk: e.matmul(
                PF[bk][:, 0:256], L["o1b"][:, :], L["zrow"][0:1, 0:256],
                start=False, stop=True), inc="tt")
            if ti == 3:
                p.free("zrow", "tt")
            p.wait(Sc, "tt")
            gsl = ti % 2
            p.acquire(Sc, f"gz{gsl}")
            p.op(Sc, lambda e, bk=bk, gsl=gsl: e.activation(
                L["gzt"][:, gsl, 0:256], PF[bk][:, 0:256], AF.Silu), inc="ss")
            p.free(f"PF{bk}", "ss")
            p.wait(V, "ss")
            p.op(V, lambda e, ti=ti, ofc=ofc, gsl=gsl: e.tensor_tensor(
                L["ytl"][:, ti, 256 * ofc:256 * ofc + 256],
                L["ytl"][:, ti, 256 * ofc:256 * ofc + 256], L["gzt"][:, gsl, 0:256],
                ALU.mult))
            p.op(V, lambda e, ti=ti, ofc=ofc, gsl=gsl: e.scalar_tensor_tensor(
                L["scrQ"][:, 0:256], L["ytl"][:, ti, 256 * ofc:256 * ofc + 256], 1.0,
                L["ytl"][:, ti, 256 * ofc:256 * ofc + 256], ALU.bypass, ALU.mult,
                accum_out=L["rsums"][:, 8 * ti + ofc:8 * ti + ofc + 1]), inc="vv")
            p.free(f"gz{gsl}", "vv")
        p.free(f"wsl{buf}", "tt")
    for ti in range(4):
        p.op(V, lambda e, ti=ti: e.reduce_sum(
            L["qcolv"][:, :], L["rsums"][:, 8 * ti:8 * ti + 8], AXX))
        p.op(V, lambda e: e.tensor_scalar_mul(L["qcolv"][:, :], L["qcolv"][:, :], 1.0 / 2048),
             inc="vv")
        p.wait(Sc, "vv")
        p.op(Sc, lambda e: e.activation(L["qcolv"][:, :], L["qcolv"][:, :], AF.Ln, bias=L["cflt"][:, 0:1]))
        p.op(Sc, lambda e: e.activation(L["qcolv"][:, :], L["qcolv"][:, :], AF.Exp, scale=-0.5),
             inc="ss")
        p.wait(V, "ss")
        p.op(V, lambda e, ti=ti: e.tensor_scalar_mul(
            L["ytl"][:, ti, :], L["ytl"][:, ti, :], L["qcolv"][:, :]), inc="vv")
    for ti in range(4):
        for cg in range(4):
            bk = cg % 2
            p.acquire(Tn, f"PB{bk}")
            p.wait(Tn, "vv")
            for i in range(4):
                ct = 4 * cg + i
                p.op(Tn, lambda e, ti=ti, ct=ct, i=i, bk=bk: e.transpose(
                    PB[bk][:, 128 * i:128 * i + 128],
                    L["ytl"][:, ti, 128 * ct:128 * ct + 128], L["idb"][:, :]),
                    inc="tt" if i == 3 else None)
            p.wait(V, "tt")
            p.op(V, lambda e, ti=ti, cg=cg, bk=bk: e.tensor_copy(
                L["yfm"][:, 4 * cg:4 * cg + 4, 128 * ti:128 * ti + 128],
                PB[bk][:, 0:512].rearrange("p (c t) -> p c t", c=4)), inc="vv")
            p.free(f"PB{bk}", "vv")
    for sl_, dstn in ((0, "abct"), (1, "bbct")):
        p.acquire(Tn, "PF4")
        for i in range(4):
            p.acquire(V, "dgs")
            p.op(V, lambda e, i=i, tq=tq, sl_=sl_: e.tensor_scalar(
                L["dgs"][:, :], L["idf"][:, :], L["pqa"][:, sl_, 4 * tq + i:4 * tq + i + 1],
                None, ALU.mult), inc="vv")
            p.wait(Tn, "vv")
            p.op(Tn, lambda e, i=i: e.matmul(PF[4][:, 128 * i:128 * i + 128],
                                             L["o128"][:, :], L["dgs"][:, :],
                                             start=True, stop=True), inc="tt")
            p.free("dgs", "tt")
        p.wait(V, "tt")
        p.op(V, lambda e, dstn=dstn: e.tensor_copy(L[dstn][:, :], PF[4][:, 0:512]), inc="vv")
        p.free("PF4", "vv")
    p.acquire(Tn, "PF4")
    p.acquire(Tn, "PF5")
    for pas in range(2):
        for o4 in range(4):
            p.acquire(Tn, f"PF{o4}")
        for k in range(16):
            wb = k % 2
            p.acquire(G, f"wosb{wb}")
            p.op(G, lambda e, k=k, pas=pas, wb=wb: e.dma_start(
                out=L["wosb"][:, wb, :],
                in_=L["d_wout"][128 * k:128 * (k + 1), 512 * pas:512 * pas + 512]),
                inc=f"lo{wb}", n=16)
            p.wait(V, f"lo{wb}")
            p.op(V, lambda e, k=k, wb=wb: e.tensor_scalar(
                L["wosb"][:, wb, :], L["wosb"][:, wb, :], L["rmsw"][:, k:k + 1],
                None, ALU.mult), inc="vv")
            p.wait(Tn, "vv")
            for o4 in range(4):
                p.op(Tn, lambda e, k=k, o4=o4, wb=wb: e.matmul(
                    PF[o4][:, 0:512], L["wosb"][:, wb, 128 * o4:128 * o4 + 128],
                    L["yfm"][:, k, :], start=(k == 0), stop=(k == 15)),
                    inc="tt" if o4 == 3 else None)
            p.free(f"wosb{wb}", "tt")
        p.wait(V, "tt")
        for o4 in range(4):
            o = 4 * pas + o4
            p.acquire("sync", "xo1")
            p.op("sync", lambda e, o=o, tq=tq: e.dma_start(
                out=L["xo1"][:, :],
                in_=L["d_xf"][128 * o:128 * (o + 1), 512 * tq + 3:512 * tq + 515]),
                inc="lx", n=16)
            p.wait(V, "lx")
            xslot = o4 % 2
            p.acquire(V, f"xnw{xslot}")
            p.op(V, lambda e: e.tensor_mul(L["scrA"][:, 0:512], L["xo1"][:, :],
                                           L["abct"][:, :]))
            p.op(V, lambda e, o4=o4: e.tensor_add(L["scrB"][:, 0:512],
                                                  L["scrA"][:, 0:512], PF[o4][:, 0:512]))
            p.op(V, lambda e, xslot=xslot: e.tensor_sub(
                L["xnw"][:, xslot, :], L["scrB"][:, 0:512], L["bbct"][:, :]), inc="vv")
            p.free("xo1", "vv")
            p.free(f"PF{o4}", "vv")
            p.wait(V, "ss")
            p.wait(V, "tt")
            p.op(V, lambda e, xslot=xslot: e.tensor_copy(L["xbt"][:, :], L["xnw"][:, xslot, :]),
                 inc="vv")
            p.wait("sync", "vv")
            p.op("sync", lambda e, o=o, tq=tq, xslot=xslot: e.dma_start(
                out=L["d_xns"][o, :, 512 * tq:512 * tq + 512], in_=L["xnw"][:, xslot, :]),
                inc=f"lz{xslot}", n=16)
            p.free(f"xnw{xslot}", f"lz{xslot}")
            p.wait(Sc, "vv")
            p.wait(Sc, "tt")
            p.op(Sc, lambda e: e.activation(L["xsq2"][:, :], L["xbt"][:, :], AF.Square),
                 inc="ss")
            p.wait(Tn, "vv")
            p.wait(Tn, "ss")
            p.op(Tn, lambda e, o=o: e.matmul(
                PF[5][0:1, 0:512], L["ocb"][:, :], L["xbt"][:, :],
                start=(o == 0), stop=(o == 7)), inc="tt")
            p.op(Tn, lambda e, o=o: e.matmul(
                PF[4][0:1, 0:512], L["ocb"][:, :], L["xsq2"][:, :],
                start=(o == 0), stop=(o == 7)),
                inc="tt")
    p.wait(V, "tt")
    rS = lambda i: L["rows"][0:1, i, 0:512]
    p.op(V, lambda e: e.tensor_scalar(rS(2), PF[5][0:1, 0:512], 1.0 / 1024, None, ALU.mult))
    p.op(V, lambda e: e.tensor_scalar(rS(3), PF[4][0:1, 0:512], 1.0 / 1024, None, ALU.mult),
         inc="vv")
    p.free("PF5", "vv")
    p.free("PF4", "vv")
    p.op(V, lambda e: e.tensor_mul(L["rowz"][0:1, :], rS(2), rS(2)))
    p.op(V, lambda e: e.tensor_sub(rS(3), rS(3), L["rowz"][0:1, :]), inc="vv")
    p.wait(Sc, "vv")
    p.op(Sc, lambda e: e.activation(rS(3), rS(3), AF.Ln, bias=L["cflt"][0:1, 0:1]))
    p.op(Sc, lambda e: e.activation(rS(3), rS(3), AF.Exp, scale=-0.5), inc="ss")
    p.wait(Tn, "ss")
    p.acquire(Tn, "PF4")
    for sl_, dc in ((2, 0), (3, 1)):
        for i in range(4):
            p.op(Tn, lambda e, sl_=sl_, i=i: e.transpose(
                PF[4][:, 4 * (sl_ - 2) + i:4 * (sl_ - 2) + i + 1],
                L["rows"][0:1, sl_, 128 * i:128 * i + 128], L["id1"][:, :]),
                inc="tt" if (sl_, i) == (3, 3) else None)
    p.wait(V, "tt")
    p.op(V, lambda e, tq=tq: e.tensor_copy(L["pq2"][:, 0, 4 * tq:4 * tq + 4], PF[4][:, 0:4]))
    p.op(V, lambda e, tq=tq: e.tensor_copy(L["pq2"][:, 1, 4 * tq:4 * tq + 4], PF[4][:, 4:8]),
         inc="vv")
    p.free("PF4", "vv")
    if "dumpyfm" in DBG_FLAGS:
        p.wait("sync", "vv")
        p.op("sync", lambda e, tq=tq: e.dma_start(
            out=L["d_yfm"][tq], in_=L["yfm"][:, :, :]), inc="ou", n=16)


def emit_ffn(nc, p, L):
    PF = L["PF"]
    V, Sc, Tn, G = "vector", "scalar", "tensor", "gpsimd"

    def load_lora(j):
        p.acquire(G, "lab1")
        p.op(G, lambda e, j=j: e.dma_start(out=L["lab1"][:, :], in_=L["d_la"][j]),
             inc="lqg", n=16)
        p.op(G, lambda e, j=j: e.dma_start(out=L["lbb1"][:, :], in_=L["d_lb"][j]),
             inc="lqg", n=16)

    def dequant(j):
        tgt = j % 2
        load_lora(j)
        p.acquire(V, f"weff{tgt}")
        for k in range(8):
            p.acquire("sync", "qsl")
            p.op("sync", lambda e, j=j, k=k: e.dma_start(
                out=L["qsl"][:, :], in_=L["d_qt"][j, 128 * k:128 * (k + 1), :]),
                inc="lq", n=16)
            p.acquire(G, "ssl")
            p.op(G, lambda e, j=j, k=k: e.dma_start(
                out=L["ssl"][:, :], in_=L["d_st"][j, 128 * k:128 * (k + 1), :]),
                inc="lqg", n=16)
            p.acquire(Sc, "t1s")
            p.wait(Sc, "lq")
            p.op(Sc, lambda e: e.activation(
                L["t1s"][:, :], L["qsl"][:, :], AF.Identity, bias=L["cflt"][:, 1:2], scale=1.0 / 7.5),
                inc="ss")
            p.free("qsl", "ss")
            p.wait(V, "ss")
            p.wait(V, "lqg")
            p.op(V, lambda e, k=k, tgt=tgt: e.tensor_tensor(
                L["weff"][:, tgt, k, :], L["t1s"][:, :], L["ssl"][:, :], ALU.mult), inc="vv")
            p.free("ssl", "vv")
            p.free("t1s", "vv")
            for half in range(2):
                p.acquire(Tn, "PF5")
                p.wait(Tn, "lqg")
                p.op(Tn, lambda e, j=j, k=k, half=half: e.matmul(
                    PF[5][:, 0:512], L["lab1"][:, 128 * k:128 * (k + 1)],
                    L["lbb1"][:, 512 * half:512 * half + 512], start=True, stop=True),
                    inc="tt")
                p.acquire(Sc, "lsc")
                p.wait(Sc, "tt")
                p.op(Sc, lambda e: e.copy(L["lsc"][:, :], PF[5][:, 0:512]), inc="ss")
                p.free("PF5", "ss")
                p.wait(V, "ss")
                p.op(V, lambda e, k=k, tgt=tgt, half=half: e.tensor_add(
                    L["weff"][:, tgt, k, 512 * half:512 * half + 512],
                    L["weff"][:, tgt, k, 512 * half:512 * half + 512],
                    L["lsc"][:, :]), inc="vv")
                p.free("lsc", "vv")
        p.bankfree[f"wrdy{tgt}"] = ("vv", p.cnt["vv"])
        p.free("lab1", "tt")

    def wait_weight(eng, tgt):
        if f"wrdy{tgt}" in p.bankfree:
            sn, v = p.bankfree[f"wrdy{tgt}"]
            p.wait(eng, sn, v)

    p.wait("sync", "lz0")
    p.wait("sync", "lz1")
    for ch in range(4):
        p.acquire("sync", "xnt2")
        p.op("sync", lambda e, ch=ch: e.dma_start(
            out=L["xnt2"][:, :, :],
            in_=L["d_xns"].rearrange("o p t -> p o t")[:, :, 512 * ch:512 * ch + 512]),
            inc="lzl", n=16)
        for sl_, dst in ((0, "mbc"), (1, "rbc")):
            p.acquire(Tn, "PF4")
            for i in range(4):
                p.acquire(V, "dgs")
                p.op(V, lambda e, sl_=sl_, ch=ch, i=i: e.tensor_scalar(
                    L["dgs"][:, :], L["idf"][:, :],
                    L["pq2"][:, sl_, 4 * ch + i:4 * ch + i + 1], None, ALU.mult), inc="vv")
                p.wait(Tn, "vv")
                p.op(Tn, lambda e, i=i: e.matmul(
                    PF[4][:, 128 * i:128 * i + 128], L["o128"][:, :], L["dgs"][:, :],
                    start=True, stop=True), inc="tt")
                p.free("dgs", "tt")
            p.wait(V, "tt")
            p.op(V, lambda e, dst=dst: e.tensor_copy(L[dst][:, :], PF[4][:, 0:512]),
                 inc="vv")
            p.free("PF4", "vv")
        p.wait(V, "lzl")
        for k in range(8):
            p.op(V, lambda e, k=k: e.tensor_sub(
                L["scrA"][:, 0:512], L["xnt2"][:, k, :], L["mbc"][:, :]))
            p.op(V, lambda e, k=k, ch=ch: e.tensor_tensor(
                L["hbuf"][:, k, 512 * ch:512 * ch + 512], L["scrA"][:, 0:512],
                L["rbc"][:, :], ALU.mult), inc="vv" if k == 7 else None)
        p.free("xnt2", "vv")

    dequant(0)
    dequant(1)
    for b in range(3):
        j1, j2 = 2 * b, 2 * b + 1
        wait_weight(Tn, 0)
        for o in range(8):
            for ch in range(4):
                p.acquire(Tn, f"PF{ch}")
            for k in range(8):
                for ch in range(4):
                    p.op(Tn, lambda e, o=o, k=k, ch=ch: e.matmul(
                        PF[ch][:, 0:512], L["weff"][:, 0, k, 128 * o:128 * o + 128],
                        L["hbuf"][:, k, 512 * ch:512 * ch + 512],
                        start=(k == 0), stop=(k == 7)),
                        inc="tt" if (k == 7 and ch == 3) else None)
            p.wait(V, "tt")
            for ch in range(4):
                p.op(V, lambda e, o=o, ch=ch, j1=j1: e.tensor_scalar_add(
                    L["y1"][:, o, 512 * ch:512 * ch + 512], PF[ch][:, 0:512],
                    L["fbias"][:, 8 * j1 + o:8 * j1 + o + 1]), inc="vv")
                p.free(f"PF{ch}", "vv")
        p.free("weff0", "tt")
        for ch in range(4):
            p.acquire(Tn, "PF4")
            p.acquire(Tn, "PF5")
            for o in range(8):
                p.wait(Sc, "vv")
                p.wait(Sc, "tt")
                p.op(Sc, lambda e, o=o, ch=ch: e.activation(
                    L["sqs"][:, :], L["y1"][:, o, 512 * ch:512 * ch + 512], AF.Square),
                    inc="ss")
                p.wait(Tn, "ss")
                p.op(Tn, lambda e, o=o, ch=ch: e.matmul(
                    PF[4][0:1, 0:512], L["ocb"][:, :],
                    L["y1"][:, o, 512 * ch:512 * ch + 512],
                    start=(o == 0), stop=(o == 7)))
                p.op(Tn, lambda e, o=o: e.matmul(
                    PF[5][0:1, 0:512], L["ocb"][:, :], L["sqs"][:, :],
                    start=(o == 0), stop=(o == 7)), inc="tt")
            p.wait(V, "tt")
            rS = lambda i: L["rows"][0:1, i, 0:512]
            p.op(V, lambda e: e.tensor_scalar(rS(2), PF[4][0:1, 0:512], 1.0 / 1024, None,
                                              ALU.mult))
            p.op(V, lambda e: e.tensor_scalar(rS(3), PF[5][0:1, 0:512], 1.0 / 1024, None,
                                              ALU.mult), inc="vv")
            p.free("PF4", "vv")
            p.free("PF5", "vv")
            p.op(V, lambda e: e.tensor_mul(L["rowz"][0:1, :], rS(2), rS(2)))
            p.op(V, lambda e: e.tensor_sub(rS(3), rS(3), L["rowz"][0:1, :]), inc="vv")
            p.wait(Sc, "vv")
            p.op(Sc, lambda e: e.activation(rS(3), rS(3), AF.Ln, bias=L["cflt"][0:1, 0:1]))
            p.op(Sc, lambda e: e.activation(rS(3), rS(3), AF.Exp, scale=-0.5), inc="ss")
            p.acquire(Tn, "PF4")
            p.acquire(Tn, "PF5")
            p.wait(Tn, "ss")
            p.op(Tn, lambda e: e.matmul(PF[4][:, 0:512], L["o1f"][:, :], rS(2),
                                        start=True, stop=True))
            p.op(Tn, lambda e: e.matmul(PF[5][:, 0:512], L["o1f"][:, :], rS(3),
                                        start=True, stop=True), inc="tt")
            p.wait(V, "tt")
            for o in range(8):
                p.wait(V, "ss")
                p.op(V, lambda e, o=o, ch=ch: e.tensor_sub(
                    L["scrA"][:, 0:512], L["y1"][:, o, 512 * ch:512 * ch + 512],
                    PF[4][:, 0:512]))
                p.op(V, lambda e, o=o: e.tensor_tensor(
                    L["scrB"][:, 0:512], L["scrA"][:, 0:512], PF[5][:, 0:512], ALU.mult),
                    inc="vv")
                p.wait(Sc, "vv")
                p.op(Sc, lambda e, o=o, ch=ch, b=b: e.activation(
                    L["y1"][:, o, 512 * ch:512 * ch + 512], L["scrB"][:, 0:512], AF.Gelu,
                    scale=L["fgam"][:, 8 * b + o:8 * b + o + 1],
                    bias=L["fbet"][:, 8 * b + o:8 * b + o + 1]), inc="ss")
            p.free("PF4", "vv")
            p.free("PF5", "vv")
        wait_weight(Tn, 1)
        p.wait(Tn, "ss")
        for o in range(8):
            for ch in range(4):
                p.acquire(Tn, f"PF{ch}")
            for k in range(8):
                for ch in range(4):
                    p.op(Tn, lambda e, o=o, k=k, ch=ch: e.matmul(
                        PF[ch][:, 0:512], L["weff"][:, 1, k, 128 * o:128 * o + 128],
                        L["y1"][:, k, 512 * ch:512 * ch + 512],
                        start=(k == 0), stop=(k == 7)),
                        inc="tt" if (k == 7 and ch == 3) else None)
            p.wait(V, "tt")
            for ch in range(4):
                p.op(V, lambda e, o=o, ch=ch, j2=j2: e.scalar_tensor_tensor(
                    L["hbuf"][:, o, 512 * ch:512 * ch + 512], PF[ch][:, 0:512],
                    L["fbias"][:, 8 * j2 + o:8 * j2 + o + 1],
                    L["hbuf"][:, o, 512 * ch:512 * ch + 512], ALU.add, ALU.add),
                    inc="vv")
                p.free(f"PF{ch}", "vv")
        p.free("weff1", "tt")
        if b < 2:
            dequant(2 * b + 2)
            dequant(2 * b + 3)
    for ch in range(4):
        p.acquire("sync", "xnt2")
        p.op("sync", lambda e, ch=ch: e.dma_start(
            out=L["xnt2"][:, :, :],
            in_=L["d_xns"].rearrange("o p t -> p o t")[:, :, 512 * ch:512 * ch + 512]),
            inc="lzl", n=16)
        p.wait(V, "lzl")
        for k in range(8):
            oslot = k % 2
            p.acquire(V, f"outt{oslot}")
            p.op(V, lambda e, k=k, ch=ch, oslot=oslot: e.tensor_add(
                L["outt"][:, oslot, :], L["xnt2"][:, k, :],
                L["hbuf"][:, k, 512 * ch:512 * ch + 512]), inc="vv")
            p.wait("sync", "vv")
            p.op("sync", lambda e, k=k, ch=ch, oslot=oslot: e.dma_start(
                out=L["d_out"][k, :, 512 * ch:512 * ch + 512], in_=L["outt"][:, oslot, :]),
                inc=f"ob{oslot}", n=16)
            p.free(f"outt{oslot}", f"ob{oslot}")
        p.free("xnt2", "vv")


def _stage(inputs):
    x = np.asarray(inputs["L0_attn_x"], np.float32)
    xf = np.ascontiguousarray(x.transpose(0, 2, 1))
    shared = {
        "w_in": np.ascontiguousarray(np.asarray(inputs["L0_attn_W_in"], np.float32)),
        "w_out": np.ascontiguousarray(np.asarray(inputs["L0_attn_W_out"], np.float32)),
        "conv_w": np.ascontiguousarray(
            np.asarray(inputs["L0_attn_conv_w"], np.float32).reshape(18, 128, 4)
            .transpose(1, 0, 2).reshape(128, 72)),
        "conv_b": np.ascontiguousarray(
            np.asarray(inputs["L0_attn_conv_b"], np.float32).reshape(18, 128).T),
        "ln_w": np.ascontiguousarray(
            np.asarray(inputs["L0_attn_ln_w"], np.float32).reshape(8, 128).T),
        "ln_b": np.ascontiguousarray(
            np.asarray(inputs["L0_attn_ln_b"], np.float32).reshape(8, 128).T),
        "dt_bias": np.ascontiguousarray(
            np.asarray(inputs["L0_attn_dt_bias"], np.float32).reshape(16, 1)),
        "a_log": np.ascontiguousarray(
            np.asarray(inputs["L0_attn_A_log"], np.float32).reshape(16, 1)),
        "d_vec": np.ascontiguousarray(
            np.asarray(inputs["L0_attn_D"], np.float32).reshape(16, 1)),
        "rms_w": np.ascontiguousarray(
            np.asarray(inputs["L0_attn_rms_w"], np.float32).reshape(16, 128).T),
        "q_t": np.ascontiguousarray(
            np.asarray(inputs["L0_ffn_qweights"], np.int32).transpose(0, 2, 1)),
        "s_t": np.ascontiguousarray(np.repeat(
            np.asarray(inputs["L0_ffn_scales"], np.float32).transpose(0, 2, 1), 16, axis=1)),
        "f_bias": np.ascontiguousarray(
            np.asarray(inputs["L0_ffn_biases"], np.float32).reshape(6, 8, 128)
            .transpose(2, 0, 1).reshape(128, 48)),
        "lora_a": np.ascontiguousarray(np.asarray(inputs["L0_ffn_lora_a"], np.float32)),
        "lora_bt": np.ascontiguousarray(
            np.asarray(inputs["L0_ffn_lora_b"], np.float32).transpose(0, 2, 1)),
        "f_gam": np.ascontiguousarray(
            np.asarray(inputs["L0_ffn_ln_gamma"], np.float32).reshape(3, 8, 128)
            .transpose(2, 0, 1).reshape(128, 24)),
        "f_bet": np.ascontiguousarray(
            np.asarray(inputs["L0_ffn_ln_beta"], np.float32).reshape(3, 8, 128)
            .transpose(2, 0, 1).reshape(128, 24)),
    }
    in_maps = []
    for c in range(8):
        b, r = c // 4, c % 4
        t0 = r * 2048
        xfc = np.zeros((1024, 2051), np.float32)
        xfc[:, 3:] = xf[b][:, t0:t0 + 2048]
        if r > 0:
            xfc[:, 0:3] = xf[b][:, t0 - 3:t0]
        mm_ = np.zeros((16, 4), np.float32)
        mm_[:, :r] = 1.0
        oh_ = np.zeros((16, 4), np.float32)
        oh_[:, max(r - 1, 0)] = 1.0
        pm_ = np.full((128, 4), 0.0 if r == 0 else 1.0, np.float32)
        m = dict(shared)
        m.update({"xf": xfc, "m_mask": mm_, "o_hot": oh_, "p_mask": pm_})
        in_maps.append(m)
    return x, in_maps


def kernel(**inputs):
    x, in_maps = _stage(inputs)
    nc = build(None)
    res = run_bass_kernel_spmd(nc, in_maps, core_ids=list(range(8)))
    out = np.empty((2, 8192, 1024), np.float32)
    for c in range(8):
        b, r = c // 4, c % 4
        of = np.asarray(res.results[c]["out"]).reshape(1024, 2048)
        out[b, r * 2048:(r + 1) * 2048, :] = of.T
    return out


# version 03
# speedup vs baseline: 1.0745x; 1.0745x over previous
import contextlib
import numpy as np
import ml_dtypes
import concourse.bass as bass
import concourse.mybir as mybir
from concourse.bass_utils import run_bass_kernel_spmd

F32 = mybir.dt.float32
BF16 = mybir.dt.bfloat16
I32 = mybir.dt.int32
AF = mybir.ActivationFunctionType
ALU = mybir.AluOpType
AXX = mybir.AxisListType.X

EPS = 1e-5
ENGS = ("sync", "tensor", "vector", "scalar", "gpsimd")
DBG_FLAGS = set()


class P:

    def __init__(self, nc, sem_names, stack):
        self.nc = nc
        self.q = {e: [] for e in ENGS}
        self.sems = {n: stack.enter_context(nc.semaphore(n)) for n in sem_names}
        self.cnt = {n: 0 for n in sem_names}
        self.nbar = 0
        self.bankfree = {}

    def emit(self, eng, fn):
        self.q[eng].append(fn)

    DRAIN_ENGS = ("vector", "scalar", "gpsimd")

    def op(self, eng, fn, inc=None, n=1):
        if inc is None:
            self.emit(eng, fn)
        else:
            self.cnt[inc] += n
            sem = self.sems[inc]
            self.emit(eng, lambda e, fn=fn, sem=sem, n=n: fn(e).then_inc(sem, n))
        if eng in self.DRAIN_ENGS:
            self.emit(eng, lambda e: e.drain())
        return self.cnt[inc] if inc else None

    def wait(self, eng, sname, val=None):
        if val is None:
            val = self.cnt[sname]
        if val <= 0:
            return
        sem = self.sems[sname]
        self.emit(eng, lambda e, sem=sem, val=val: e.wait_ge(sem, val))

    def free(self, res, sname):
        self.bankfree[res] = (sname, self.cnt[sname])

    def acquire(self, eng, res):
        if res in self.bankfree:
            sname, v = self.bankfree[res]
            self.wait(eng, sname, v)
            del self.bankfree[res]

    def barrier(self):
        self.nbar += 1
        tgt = 5 * self.nbar
        sem = self.sems["bar"]
        for e in ENGS:
            self.emit(e, lambda en, sem=sem: en.nop().then_inc(sem, 1))
            self.emit(e, lambda en, sem=sem, tgt=tgt: en.wait_ge(sem, tgt))


def build(dbg_upto=None):
    import os
    nc = bass.Bass("TRN2", target_bir_lowering=False,
                   detect_race_conditions=bool(os.environ.get("KRACE")))
    L = {}

    L["d_xf"] = nc.dram_tensor("xf", [1024, 2051], F32, kind="ExternalInput").ap()
    L["d_win"] = nc.dram_tensor("w_in", [1024, 4368], F32, kind="ExternalInput").ap()
    L["d_wout"] = nc.dram_tensor("w_out", [2048, 1024], F32, kind="ExternalInput").ap()
    for n, shp, dt in [("conv_w", [128, 72], F32), ("conv_b", [128, 18], F32),
                       ("ln_w", [128, 8], F32), ("ln_b", [128, 8], F32),
                       ("dt_bias", [16, 1], F32), ("a_log", [16, 1], F32),
                       ("d_vec", [16, 1], F32), ("rms_w", [128, 16], F32),
                       ("f_bias", [128, 48], F32), ("f_gam", [128, 24], F32),
                       ("f_bet", [128, 24], F32), ("m_mask", [16, 4], F32),
                       ("o_hot", [16, 4], F32), ("p_mask", [128, 4], F32)]:
        L["d_" + n] = nc.dram_tensor(n, shp, dt, kind="ExternalInput").ap()
    L["d_qt"] = nc.dram_tensor("q_t", [6, 1024, 1024], I32, kind="ExternalInput").ap()
    L["d_st"] = nc.dram_tensor("s_t", [6, 1024, 1024], F32, kind="ExternalInput").ap()
    L["d_la"] = nc.dram_tensor("lora_a", [6, 32, 1024], F32, kind="ExternalInput").ap()
    L["d_lb"] = nc.dram_tensor("lora_bt", [6, 32, 1024], F32, kind="ExternalInput").ap()

    L["d_out"] = nc.dram_tensor("out", [8, 128, 2048], F32, kind="ExternalOutput").ap()
    dbgk = "ExternalOutput" if dbg_upto else "Internal"
    L["d_ysc"] = nc.dram_tensor("yscr", [16, 128, 2048], BF16, kind=dbgk).ap()
    L["d_xns"] = nc.dram_tensor("xnscr", [8, 128, 2048], F32, kind=dbgk).ap()
    agi_t = nc.dram_tensor("agin", [128, 2052], F32)
    ago_t = nc.dram_tensor("agout", [512, 2052], F32)
    L["d_agi"], L["d_ago"] = agi_t.ap(), ago_t.ap()
    L["agi_t"], L["ago_t"] = agi_t, ago_t
    if dbg_upto:
        L["d_dxnb"] = nc.dram_tensor("d_xnb", [128, 8, 2056], BF16, kind="ExternalOutput").ap()
        L["d_dxbc"] = nc.dram_tensor("d_xbc", [4, 128, 18, 520], BF16, kind="ExternalOutput").ap()
        L["d_ddt"] = nc.dram_tensor("d_dt", [16, 2048], BF16, kind="ExternalOutput").ap()
        L["d_dlag"] = nc.dram_tensor("d_lag", [16, 2048], F32, kind="ExternalOutput").ap()
        L["d_msc"] = nc.dram_tensor("d_msc", [128, 32], F32, kind="ExternalOutput").ap()
        L["d_yfm"] = nc.dram_tensor("d_yfm", [4, 128, 16, 512], BF16, kind="ExternalOutput").ap()

    consts = {
        "idb": np.eye(128, dtype=ml_dtypes.bfloat16),
        "id16": np.eye(16, dtype=np.float32),
        "idf": np.eye(128, dtype=np.float32),
        "utb": np.triu(np.ones((128, 128), np.float32)).astype(ml_dtypes.bfloat16),
        "o1f": np.ones((1, 128), np.float32),
        "o1b": np.ones((1, 128), ml_dtypes.bfloat16),
        "ocb": np.ones((128, 1), ml_dtypes.bfloat16),
        "o16": np.ones((16, 128), np.float32),
        "o128": np.ones((128, 128), np.float32),
        "id1": np.ones((1, 1), np.float32),
        "id16b": np.eye(16, dtype=ml_dtypes.bfloat16),
        "cflt": np.tile(np.array([[EPS, -1.0, 1.0, 0.0]], np.float32), (128, 1)),
    }
    eh = np.zeros((16, 2048), np.float32)
    for h in range(16):
        eh[h, h * 128:(h + 1) * 128] = 1.0
    consts["ehs"] = eh
    sel = np.zeros((128, 128), np.float32)
    sel[127, :] = 1.0
    consts["sls"] = sel
    cdr = {k: nc.inline_tensor(np.ascontiguousarray(v), "c_" + k) for k, v in consts.items()}

    sem_names = ["bar", "px", "pg", "lx", "lw0", "lw1", "lq", "lqg",
                 "ly0", "ly1", "ly2", "ly3", "lz0", "lz1", "lzl", "lo0", "lo1",
                 "lg", "lgg", "tt", "vv", "ss", "gg", "cc", "ou",
                 "ob0", "ob1", "oud"]

    with contextlib.ExitStack() as perm:
        E = perm.enter_context
        p = P(nc, sem_names, perm)
        L["p"] = p

        PF = [E(nc.psum_tensor(f"pf{i}", [128, 512], F32)) for i in range(6)]
        PB = [E(nc.psum_tensor(f"pb{i}", [128, 512], BF16)) for i in range(2)]
        L["PF"], L["PB"] = PF, PB

        for n, shp, dt in [
            ("cw", [128, 18, 4], F32), ("cb", [128, 18], F32), ("lnw", [128, 8], F32),
            ("dtb", [16, 1], F32), ("alog", [16, 1], F32), ("dv", [16, 1], F32),
            ("rmsw", [128, 16], F32), ("fbias", [128, 48], F32), ("fgam", [128, 24], F32),
            ("fbet", [128, 24], F32), ("mmask", [16, 4], F32), ("ohot", [16, 4], F32),
            ("pmask", [128, 4], F32),
            ("idb", [128, 128], BF16), ("id16", [16, 16], F32), ("idf", [128, 128], F32),
            ("utb", [128, 128], BF16), ("o1f", [1, 128], F32), ("o1b", [1, 128], BF16),
            ("ocb", [128, 1], BF16), ("o16", [16, 128], F32), ("o128", [128, 128], F32),
            ("ehs", [16, 2048], F32), ("sls", [128, 128], F32), ("id1", [1, 1], F32),
            ("o16w", [16, 512], F32), ("acol", [16, 1], F32), ("dbc", [128, 16], F32),
            ("dtbe", [16, 1], F32), ("lnbp", [128, 19], F32),
            ("pqa", [128, 2, 17], F32), ("pq2", [128, 2, 17], F32),
            ("rows", [1, 5, 520], F32), ("rowz", [1, 512], F32),
            ("scrA", [128, 512], F32), ("scrB", [128, 512], F32),
            ("scrQ", [128, 512], BF16), ("sdt", [16, 512], F32), ("id16b", [16, 16], BF16),
            ("cflt", [128, 4], F32), ("zrow", [1, 512], BF16), ("puf", [128, 128], F32),
        ]:
            L[n] = E(nc.sbuf_tensor(n, shp, dt))
        L["cdr"] = cdr

        mam = contextlib.ExitStack()
        M = mam.enter_context
        for n, shp, dt in [
            ("xnb", [128, 8, 2056], BF16), ("wsl", [128, 2, 8, 256], BF16),
            ("xbc", [128, 18, 520], BF16), ("stash", [128, 18, 4], BF16),
            ("ck", [128, 2048], BF16), ("dts", [16, 2048], BF16), ("lag", [16, 2048], F32),
            ("sdtF", [16, 512], F32),
            ("vsc", [16, 512], F32), ("H", [128, 16, 128], F32), ("Ht", [128, 16, 128], BF16),
            ("xtl", [128, 16, 128], BF16),
            ("dxh", [128, 16, 128], BF16), ("Gm", [128, 128], BF16), ("Bpp", [128, 128], BF16),
            ("Ps", [128, 2, 128], BF16), ("Ss", [128, 2, 128], BF16),
            ("laT", [128, 16], F32), ("nlaT", [128, 16], F32), ("dtT", [128, 16], BF16),
            ("lq0", [128, 16], F32), ("lq1", [128, 16], F32), ("lqz", [128, 16], F32),
            ("tmpa", [128, 16], F32), ("tmpb", [128, 16], F32), ("tmpc", [128, 16], F32),
            ("rvt", [128, 16], F32), ("avt", [128, 16], F32), ("wvt", [128, 16], F32),
            ("x4", [128, 8, 132], F32), ("xb4", [128, 8, 132], BF16),
            ("xq4", [128, 8, 132], BF16),
            ("ytl", [128, 4, 2048], BF16), ("yfm", [128, 16, 512], BF16),
            ("sinb", [128, 16, 128], BF16),
            ("tcol", [16, 4], F32), ("theta", [16, 4], F32), ("thv", [128, 16], F32),
            ("cva", [128, 512], BF16), ("cvb", [128, 512], BF16), ("cvc", [128, 512], BF16),
            ("gzt", [128, 2, 512], BF16), ("rsums", [128, 32], F32), ("qcolv", [128, 1], F32),
            ("wosb", [128, 2, 512], BF16),
            ("abct", [128, 512], F32), ("bbct", [128, 512], F32), ("dgs", [128, 128], F32),
            ("xo1", [128, 512], F32), ("xnw", [128, 2, 512], F32),
            ("xbt", [128, 512], BF16), ("xsq2", [128, 512], BF16),
        ]:
            L[n] = M(nc.sbuf_tensor(n, shp, dt))

        emit_params(nc, p, L)
        p.barrier()
        nq_run = 0 if dbg_upto == "par" else (1 if "q1" in DBG_FLAGS else 4)
        for q in range(nq_run):
            emit_a1(nc, p, L, q)
            if dbg_upto == "a1":
                continue
            emit_a2(nc, p, L, q)
            p.barrier()
            if dbg_upto == "a2":
                continue
            emit_conv(nc, p, L, q)
            p.barrier()
            if dbg_upto == "conv":
                p.op("sync", lambda e, q=q: e.dma_start(
                    out=L["d_dxbc"][q], in_=L["xbc"][:, :, :]), inc="ou", n=16)
                p.barrier()
                continue
            for kq in range(4):
                emit_ssm_chunk(nc, p, L, 4 * q + kq)
            p.barrier()
        if dbg_upto:
            p.barrier()
            p.op("sync", lambda e: e.dma_start(out=L["d_dxnb"], in_=L["xnb"][:, :, :]), inc="ou", n=16)
            p.op("sync", lambda e: e.dma_start(out=L["d_ddt"], in_=L["dts"][:, :]), inc="ou", n=16)
            p.op("sync", lambda e: e.dma_start(out=L["d_dlag"], in_=L["lag"][:, :]), inc="ou", n=16)
            p.op("vector", lambda e: e.tensor_copy(L["scrA"][:, 0:4], L["pmask"][:, :]))
            p.op("vector", lambda e: e.tensor_copy(L["scrA"][:, 4:23], L["lnbp"][:, :]), inc="vv")
            p.wait("sync", "vv")
            p.op("sync", lambda e: e.dma_start(out=L["d_msc"], in_=L["scrA"][:, 0:32]), inc="oud", n=16)
            for e_ in ENGS:
                p.wait(e_, "oud")
                p.wait(e_, "ou")
        if dbg_upto in ("par", "a1", "a2"):
            dbg_upto = "conv"
        if dbg_upto not in ("conv", "ssm"):
            emit_ag(nc, p, L)
            p.barrier()
            for kg in range(4 if "q1" in DBG_FLAGS else 16):
                emit_corr(nc, p, L, kg)
            p.barrier()
        if dbg_upto in (None, "tail", "ffn"):
            for tq in range(1 if "q1" in DBG_FLAGS else 4):
                emit_tail(nc, p, L, tq)
                p.barrier()
        mam.close()

        if dbg_upto in (None, "ffn"):
            ffn = contextlib.ExitStack()
            Fh = ffn.enter_context
            for n, shp, dt in [
                ("hbuf", [128, 8, 2048], BF16), ("y1", [128, 8, 2048], BF16),
                ("weff", [128, 2, 8, 1024], BF16), ("qsl", [128, 1024], I32),
                ("ssl", [128, 1024], BF16), ("t1s", [128, 1024], BF16),
                ("lab1", [32, 1024], BF16), ("lbb1", [32, 1024], BF16),
                ("lsc", [128, 512], BF16), ("xnt2", [128, 8, 512], F32),
                ("mbc", [128, 512], BF16), ("rbc", [128, 512], BF16),
                ("outt", [128, 2, 512], F32), ("sqs", [128, 512], BF16),
            ]:
                L[n] = Fh(nc.sbuf_tensor(n, shp, dt))
            emit_ffn(nc, p, L)
            ffn.close()

        p.wait("gpsimd", "ou")
        p.wait("gpsimd", "ob0")
        p.wait("gpsimd", "ob1")
        p.barrier()

        with nc.Block() as block:
            @block.sync
            def _(e):
                for f in p.q["sync"]:
                    f(e)

            @block.tensor
            def _(e):
                for f in p.q["tensor"]:
                    f(e)

            @block.vector
            def _(e):
                for f in p.q["vector"]:
                    f(e)

            @block.scalar
            def _(e):
                for f in p.q["scalar"]:
                    f(e)

            @block.gpsimd
            def _(e):
                for f in p.q["gpsimd"]:
                    f(e)

    return nc



def emit_params(nc, p, L):
    cdr = L["cdr"]
    pairs = [("cw", None), ("cb", "d_conv_b"), ("lnw", "d_ln_w"), ("dtb", "d_dt_bias"),
             ("alog", "d_a_log"), ("dv", "d_d_vec"), ("rmsw", "d_rms_w"),
             ("fbias", "d_f_bias"), ("fgam", "d_f_gam"), ("fbet", "d_f_bet"),
             ("mmask", "d_m_mask"), ("ohot", "d_o_hot"), ("pmask", "d_p_mask")]
    p.op("sync", lambda e: e.dma_start(
        out=L["cw"][:, :, :], in_=L["d_conv_w"].rearrange("p (a b) -> p a b", a=18)),
        inc="px", n=16)
    for dst, src in pairs[1:]:
        p.op("sync", lambda e, dst=dst, src=src: e.dma_start(
            out=L[dst][:, :], in_=L[src]), inc="px", n=16)
    for n in ["idb", "id16", "idf", "utb", "o1f", "o1b", "ocb", "o16", "o128",
              "ehs", "sls", "id1", "id16b", "cflt"]:
        p.op("sync", lambda e, n=n: e.dma_start(out=L[n][:, :], in_=cdr[n].ap()),
             inc="px", n=16)
    p.wait("vector", "px")
    p.wait("scalar", "px")
    V = "vector"
    p.op(V, lambda e: e.memset(L["xnb"][:, :, :], 0.0))
    p.op(V, lambda e: e.memset(L["xbc"][:, :, :], 0.0))
    p.op(V, lambda e: e.memset(L["dts"][:, :], 0.0))
    p.op(V, lambda e: e.memset(L["lag"][:, :], 0.0))
    p.op(V, lambda e: e.memset(L["H"][:, :, :], 0.0))
    p.op(V, lambda e: e.memset(L["Ht"][:, :, :], 0.0))
    p.op(V, lambda e: e.memset(L["lqz"][:, :], 0.0))
    p.op(V, lambda e: e.memset(L["o16w"][:, :], 1.0))
    p.op(V, lambda e: e.memset(L["rsums"][:, :], 0.0))
    p.op(V, lambda e: e.memset(L["sinb"][:, :, :], 0.0))
    p.op(V, lambda e: e.memset(L["lnbp"][:, :], 0.0))
    p.op(V, lambda e: e.tensor_copy(L["dtbe"][:, :], L["dtb"][:, :]), inc="vv")
    p.wait("gpsimd", "vv")
    p.op("gpsimd", lambda e: e.dma_start(
        out=L["xnb"][:, :, 2051:2052], in_=L["d_ln_b"].rearrange("p (g o) -> p g o", o=1),
        allow_slow_non_contiguous=True), inc="pg", n=16)
    p.wait("tensor", "pg")
    p.wait("vector", "pg")
    p.op("scalar", lambda e: e.activation(L["acol"][:, :], L["alog"][:, :], AF.Exp), inc="ss")
    p.wait(V, "ss")
    p.op(V, lambda e: e.tensor_scalar_mul(L["acol"][:, :], L["acol"][:, :], -1.0), inc="vv")
    p.op(V, lambda e: e.tensor_scalar(L["tmpa"][0:16, :], L["id16"][:, :],
                                      L["dv"][:, :], None, ALU.mult), inc="vv")
    p.wait("tensor", "vv")
    p.op("tensor", lambda e: e.matmul(L["PF"][5][0:128, 0:16], L["o16"][:, :],
                                      L["tmpa"][0:16, 0:16], start=True, stop=True), inc="tt")
    p.wait(V, "tt")
    p.op(V, lambda e: e.tensor_copy(L["dbc"][:, :], L["PF"][5][0:128, 0:16]), inc="vv")
    p.free("PF5", "vv")


def emit_a1(nc, p, L, q):
    PF = L["PF"]
    rows, rowz = L["rows"], L["rowz"]
    V, Sc, Tn = "vector", "scalar", "tensor"
    qc0 = 0 if q == 0 else 512 * q + 3
    qc1 = 512 * q + 515
    if q == 0:
        chunks = [(0, 132), (132, 260), (260, 388), (388, 516)]
    else:
        chunks = [(qc0 + 128 * i, qc0 + 128 * (i + 1)) for i in range(4)]
    xfr = L["d_xf"].rearrange("(g p) t -> p g t", g=8)
    for (c0, c1) in chunks:
        n = c1 - c0
        r0 = c0 - qc0
        rS = lambda i, r0=r0, n=n: rows[0:1, i, r0:r0 + n]
        rZ = lambda n=n: rowz[0:1, 0:n]
        p.acquire("sync", "x4")
        p.op("sync", lambda e, c0=c0, c1=c1, n=n: e.dma_start(
            out=L["x4"][:, :, 0:n], in_=xfr[:, :, c0:c1]), inc="lx", n=16)
        p.wait(V, "lx")
        p.acquire(V, "xb4")
        p.op(V, lambda e, n=n: e.tensor_copy(L["xb4"][:, :, 0:n], L["x4"][:, :, 0:n]), inc="vv")
        p.wait(Sc, "vv")
        p.acquire(Sc, "xq4")
        p.op(Sc, lambda e, n=n: e.activation(
            L["xq4"][:, :, 0:n], L["xb4"][:, :, 0:n], AF.Square), inc="ss")
        p.acquire(Tn, "PF0")
        p.acquire(Tn, "PF1")
        p.wait(Tn, "vv")
        p.wait(Tn, "ss")
        for g in range(8):
            p.op(Tn, lambda e, g=g, n=n: e.matmul(
                PF[0][0:1, 0:n], L["ocb"][:, :], L["xb4"][:, g, 0:n],
                start=(g == 0), stop=(g == 7)))
        for g in range(8):
            p.op(Tn, lambda e, g=g, n=n: e.matmul(
                PF[1][0:1, 0:n], L["ocb"][:, :], L["xq4"][:, g, 0:n],
                start=(g == 0), stop=(g == 7)), inc="tt" if g == 7 else None)
        p.free("xb4", "tt")
        p.free("xq4", "tt")
        p.wait(V, "tt")
        p.op(V, lambda e, n=n, rS=rS: e.tensor_scalar(rS(2), PF[0][0:1, 0:n], 1.0 / 1024, None, ALU.mult))
        p.op(V, lambda e, n=n, rS=rS: e.tensor_scalar(rS(3), PF[1][0:1, 0:n], 1.0 / 1024, None, ALU.mult),
             inc="vv")
        p.free("PF0", "vv")
        p.free("PF1", "vv")
        p.op(V, lambda e, rS=rS, rZ=rZ: e.tensor_mul(rZ(), rS(2), rS(2)))
        p.op(V, lambda e, rS=rS, rZ=rZ: e.tensor_sub(rS(3), rS(3), rZ()), inc="vv")
        p.wait(Sc, "vv")
        p.op(Sc, lambda e, rS=rS, rZ=rZ: e.activation(rZ(), rS(3), AF.Ln, bias=L["cflt"][0:1, 0:1]))
        p.op(Sc, lambda e, rZ=rZ: e.activation(rZ(), rZ(), AF.Exp, scale=-0.5), inc="ss")
        p.wait(V, "ss")
        p.op(V, lambda e, rS=rS, rZ=rZ: e.tensor_scalar_add(rS(0), rZ(), 1.0))
        p.op(V, lambda e, rS=rS, rZ=rZ: e.tensor_mul(rS(1), rS(2), rZ()))
        p.op(V, lambda e, rS=rS: e.tensor_scalar_add(rS(4), rS(3), EPS))
        p.op(V, lambda e, rS=rS: e.reciprocal(rS(4), rS(4)))
        p.op(V, lambda e, rS=rS: e.tensor_mul(rS(4), rS(3), rS(4)), inc="vv")
        p.wait(Sc, "vv")
        p.op(Sc, lambda e, rS=rS: e.activation(rS(4), rS(4), AF.Ln, bias=L["cflt"][0:1, 0:1]))
        p.op(Sc, lambda e, rS=rS: e.activation(rS(4), rS(4), AF.Exp, scale=-0.5), inc="ss")
        p.wait(V, "ss")
        p.op(V, lambda e, rS=rS, rZ=rZ: e.tensor_mul(rS(4), rS(4), rZ()))
        p.op(V, lambda e, rS=rS: e.tensor_mul(rS(3), rS(2), rS(4)), inc="vv")
        p.acquire(Tn, "PF2")
        p.acquire(Tn, "PF3")
        p.wait(Tn, "vv")
        p.op(Tn, lambda e, n=n, rS=rS: e.matmul(PF[2][:, 0:n], L["o1f"][:, :], rS(4),
                                         start=True, stop=True))
        p.op(Tn, lambda e, n=n, rS=rS: e.matmul(PF[3][:, 0:n], L["o1f"][:, :], rS(3),
                                         start=True, stop=True), inc="tt")
        p.wait(V, "tt")
        for g in range(8):
            p.op(V, lambda e, g=g, n=n: e.tensor_mul(
                L["scrA"][:, 0:n], L["x4"][:, g, 0:n], PF[2][:, 0:n]))
            p.op(V, lambda e, g=g, n=n, c0=c0: e.tensor_sub(
                L["xnb"][:, g, c0:c0 + n], L["scrA"][:, 0:n], PF[3][:, 0:n]),
                inc="vv" if g == 7 else None)
        p.free("PF2", "vv")
        p.free("PF3", "vv")
        p.free("x4", "vv")
    for g in range(8):
        p.op(V, lambda e, g=g, qc0=qc0, qc1=qc1: e.tensor_scalar_mul(
            L["xnb"][:, g, qc0:qc1], L["xnb"][:, g, qc0:qc1], L["lnw"][:, g:g + 1]),
            inc="vv" if g == 7 else None)
    p.acquire(Tn, "PF4")
    for sl_, dc in ((0, 0), (1, 1)):
        for i in range(4):
            roff = 512 * q + 128 * i + 3 - qc0
            p.op(Tn, lambda e, sl_=sl_, i=i, roff=roff: e.transpose(
                PF[4][:, 4 * sl_ + i:4 * sl_ + i + 1],
                rows[0:1, sl_, roff:roff + 128], L["id1"][:, :]),
                inc="tt" if (sl_, i) == (1, 3) else None)
    p.wait(V, "tt")
    p.op(V, lambda e, q=q: e.tensor_copy(L["pqa"][:, 0, 4 * q:4 * q + 4], PF[4][:, 0:4]))
    p.op(V, lambda e, q=q: e.tensor_copy(L["pqa"][:, 1, 4 * q:4 * q + 4], PF[4][:, 4:8]),
         inc="vv")
    p.free("PF4", "vv")


def emit_a2(nc, p, L, q):
    PF = L["PF"]
    xnb, wsl, xbc = L["xnb"], L["wsl"], L["xbc"]
    V, Sc, Tn = "vector", "scalar", "tensor"
    c0m = 512 * q + 3
    p.wait(Tn, "vv")
    groups = [(i, 2048 + 256 * i, 256, [2 * i, 2 * i + 1]) for i in range(9)]
    if "nodt" not in DBG_FLAGS:
        groups.append((9, 4352, 16, [18]))
    wfr = L["d_win"].rearrange("(k p) c -> p k c", k=8)
    for gi, w0, ncol, ofs in groups:
        buf = gi % 2
        p.acquire("gpsimd", f"wsl{buf}")
        if ncol == 256:
            p.op("gpsimd", lambda e, w0=w0, ncol=ncol, buf=buf: e.dma_start(
                out=wsl[:, buf, :, 0:ncol],
                in_=wfr[:, :, w0:w0 + ncol]), inc=f"lw{buf}", n=16)
        else:
            for k in range(8):
                p.op("gpsimd", lambda e, k=k, w0=w0, ncol=ncol, buf=buf: e.dma_start(
                    out=wsl[:, buf, k, 0:ncol],
                    in_=L["d_win"][128 * k:128 * (k + 1), w0:w0 + ncol]), inc=f"lw{buf}", n=16)
        p.wait(Tn, f"lw{buf}")
        if "nomm" in DBG_FLAGS:
            continue
        for ofl, of in enumerate(ofs):
            isdt = (of == 18)
            wc0 = 128 * ofl if not isdt else 0
            wcn = 128 if not isdt else 16
            pm = PF[ofl % 2]
            p.acquire(Tn, f"PF{ofl % 2}")
            orow = 16 if isdt else 128
            for k in range(8):
                p.op(Tn, lambda e, k=k, buf=buf, wc0=wc0, wcn=wcn, pm=pm, c0m=c0m, orow=orow:
                     e.matmul(pm[0:orow, 0:512], wsl[:, buf, k, wc0:wc0 + wcn],
                              xnb[:, k, c0m:c0m + 512], start=(k == 0), stop=(k == 7)),
                     inc="tt" if k == 7 else None)
            if q == 0 and "noprefix" not in DBG_FLAGS:
                pp = PF[2 + ofl % 2]
                p.acquire(Tn, f"PF{2 + ofl % 2}")
                for k in range(8):
                    p.op(Tn, lambda e, k=k, buf=buf, wc0=wc0, wcn=wcn, pp=pp, orow=orow:
                         e.matmul(pp[0:orow, 0:128], wsl[:, buf, k, wc0:wc0 + wcn],
                                  xnb[:, k, 0:128], start=(k == 0), stop=False))
                for k in range(8):
                    p.op(Tn, lambda e, k=k, buf=buf, wc0=wc0, wcn=wcn, pp=pp, orow=orow:
                         e.matmul(pp[0:orow, 128:256], wsl[:, buf, k, wc0:wc0 + wcn],
                                  xnb[:, k, 1924:2052], start=False, stop=(k == 7)),
                         inc="tt" if k == 7 else None)
            p.wait(V, "tt")
            if "noevac" in DBG_FLAGS:
                p.op(V, lambda e, pm=pm: e.tensor_copy(L["scrA"][0:1, 0:4], pm[0:1, 0:4]),
                     inc="vv")
                p.free(f"PF{ofl % 2}", "vv")
                if q == 0 and "noprefix" not in DBG_FLAGS:
                    p.free(f"PF{2 + ofl % 2}", "vv")
                continue
            if not isdt:
                if q == 0 and "noprefix" not in DBG_FLAGS:
                    pp = PF[2 + ofl % 2]
                    p.op(V, lambda e, of=of, pp=pp: e.tensor_copy(
                        L["lnbp"][:, of:of + 1], pp[0:128, 255:256]))
                    p.op(V, lambda e, of=of, pp=pp: e.tensor_scalar_add(
                        xbc[:, of, 0:3], pp[0:128, 0:3], L["lnbp"][:, of:of + 1]))
                    p.op(V, lambda e, of=of: e.tensor_mul(
                        xbc[:, of, 0:3], xbc[:, of, 0:3], L["pmask"][:, 0:3]), inc="vv")
                    p.free(f"PF{2 + ofl % 2}", "vv")
                if "evacsc" in DBG_FLAGS:
                    p.wait(Sc, "tt")
                    p.op(Sc, lambda e, of=of, pm=pm: e.activation(
                        xbc[:, of, 3:515], pm[0:128, 0:512], AF.Identity,
                        bias=L["lnbp"][:, of:of + 1]), inc="ss")
                    p.free(f"PF{ofl % 2}", "ss")
                    p.op(V, lambda e: e.nop(), inc="vv")
                elif "evacscr" in DBG_FLAGS:
                    p.op(V, lambda e, of=of, pm=pm: e.tensor_scalar_add(
                        L["scrQ"][:, 0:512], pm[0:128, 0:512], L["lnbp"][:, of:of + 1]), inc="vv")
                    p.free(f"PF{ofl % 2}", "vv")
                else:
                    p.op(V, lambda e, of=of, pm=pm: e.tensor_scalar_add(
                        xbc[:, of, 3:515], pm[0:128, 0:512], L["lnbp"][:, of:of + 1]), inc="vv")
                    p.free(f"PF{ofl % 2}", "vv")
            else:
                if q == 0 and "noprefix" not in DBG_FLAGS:
                    pp = PF[2 + ofl % 2]
                    p.op(V, lambda e, pp=pp: e.tensor_copy(
                        L["lnbp"][0:16, 18:19], pp[0:16, 255:256]))
                    p.op(V, lambda e: e.tensor_add(
                        L["dtbe"][:, :], L["dtb"][:, :], L["lnbp"][0:16, 18:19]), inc="vv")
                    p.free(f"PF{2 + ofl % 2}", "vv")
                    p.wait(Sc, "vv")
                p.wait(Sc, "tt")
                p.op(Sc, lambda e, pm=pm: e.activation(
                    L["sdt"][:, :], pm[0:16, 0:512], AF.Exp, bias=L["dtbe"][:, :]), inc="ss")
                p.wait(V, "ss")
                p.op(V, lambda e: e.tensor_scalar_add(L["sdt"][:, :], L["sdt"][:, :], 1.0),
                     inc="vv")
                p.free(f"PF{ofl % 2}", "ss")
                p.wait(Sc, "vv")
                p.op(Sc, lambda e: e.activation(
                    L["sdtF"][:, :], L["sdt"][:, :], AF.Ln), inc="ss")
                p.wait(V, "ss")
                p.op(V, lambda e, q=q: e.tensor_copy(
                    L["dts"][:, 512 * q:512 * q + 512], L["sdtF"][:, :]), inc="vv")
        p.free(f"wsl{buf}", "tt")
    p.wait(V, "ss")
    p.op(V, lambda e, q=q: e.tensor_scalar(
        L["vsc"][:, :], L["sdtF"][:, :], L["acol"][:, :], None, ALU.mult))
    if q == 0:
        p.op(V, lambda e: e.tensor_tensor_scan(
            L["lag"][:, 0:512], L["o16w"][:, :], L["vsc"][:, :], 0.0,
            ALU.mult, ALU.add), inc="vv")
    else:
        p.op(V, lambda e, q=q: e.tensor_tensor_scan(
            L["lag"][:, 512 * q:512 * q + 512], L["o16w"][:, :], L["vsc"][:, :],
            L["lag"][:, 512 * q - 1:512 * q], ALU.mult, ALU.add), inc="vv")


def emit_conv(nc, p, L, q):
    xbc, cw, cb = L["xbc"], L["cw"], L["cb"]
    V, G, Sc = "vector", "gpsimd", "scalar"
    if q > 0:
        p.op(V, lambda e: e.tensor_copy(xbc[:, :, 0:3], L["stash"][:, :, 0:3]), inc="vv")
    p.op(V, lambda e: e.tensor_copy(L["stash"][:, :, 0:3], xbc[:, :, 512:515]), inc="vv")
    p.wait(G, "vv")
    for ct in range(18):
        eng = V if ct % 3 else G
        if eng == V:
            ta, tb, tc = L["cva"][:, :], L["cvb"][:, :], L["cvc"][:, :]
            isem = "vv"
        else:
            ta, tb, tc = L["gzt"][:, 0, :], L["gzt"][:, 1, :], L["scrQ"][:, :]
            isem = "gg"
        p.acquire(eng, f"cvsc_{isem}")
        w0, w1 = cw[:, ct, 0:1], cw[:, ct, 1:2]
        w2, w3 = cw[:, ct, 2:3], cw[:, ct, 3:4]
        p.op(eng, lambda e, ct=ct, ta=ta, w0=w0: e.tensor_scalar(
            ta, xbc[:, ct, 0:512], w0, None, ALU.mult))
        p.op(eng, lambda e, ct=ct, tb=tb, w1=w1: e.tensor_scalar(
            tb, xbc[:, ct, 1:513], w1, None, ALU.mult))
        p.op(eng, lambda e, ta=ta, tb=tb: e.tensor_add(ta, ta, tb))
        p.op(eng, lambda e, ct=ct, tb=tb, w2=w2: e.tensor_scalar(
            tb, xbc[:, ct, 2:514], w2, None, ALU.mult))
        p.op(eng, lambda e, ct=ct, tc=tc, w3=w3: e.tensor_scalar(
            tc, xbc[:, ct, 3:515], w3, None, ALU.mult))
        p.op(eng, lambda e, tb=tb, tc=tc: e.tensor_add(tb, tb, tc))
        p.op(eng, lambda e, ta=ta, tb=tb: e.tensor_add(ta, ta, tb), inc=isem)
        p.wait(Sc, isem)
        p.op(Sc, lambda e, ct=ct, ta=ta: e.activation(
            xbc[:, ct, 3:515], ta, AF.Silu, bias=cb[:, ct:ct + 1]), inc="ss")
        p.free(f"cvsc_{isem}", "ss")
    p.wait(V, "ss")
    p.wait(G, "ss")


def emit_ssm_chunk(nc, p, L, kg):
    PF, PB = L["PF"], L["PB"]
    xbc, lag, dts = L["xbc"], L["lag"], L["dts"]
    V, Sc, Tn = "vector", "scalar", "tensor"
    t0 = 128 * kg
    cc = slice(3 + 128 * (kg % 4), 3 + 128 * (kg % 4) + 128)
    lqp = L["lqz"] if kg == 0 else (L["lq0"] if (kg - 1) % 2 == 0 else L["lq1"])
    lqc = L["lq0"] if kg % 2 == 0 else L["lq1"]
    p.acquire(Tn, "PF4")
    p.acquire(Tn, "PB0")
    p.op(Tn, lambda e: e.transpose(PF[4][:, 0:16], lag[:, t0:t0 + 128], L["id16"][:, :]))
    p.op(Tn, lambda e: e.transpose(PB[0][:, 0:16], dts[:, t0:t0 + 128], L["id16b"][:, :]),
         inc="tt")
    p.wait(V, "tt")
    p.acquire(V, "laT")
    p.op(V, lambda e: e.tensor_copy(L["laT"][:, :], PF[4][:, 0:16]))
    p.op(V, lambda e: e.tensor_copy(L["dtT"][:, :], PB[0][:, 0:16]))
    p.op(V, lambda e: e.tensor_scalar_mul(L["nlaT"][:, :], L["laT"][:, :], -1.0), inc="vv")
    p.free("PF4", "vv")
    p.free("PB0", "vv")
    p.acquire(Tn, "PF5")
    p.wait(Tn, "vv")
    p.op(Tn, lambda e: e.matmul(PF[5][0:128, 0:16], L["sls"][:, :], L["laT"][:, :],
                                start=True, stop=True), inc="tt")
    p.wait(V, "tt")
    p.op(V, lambda e, lqc=lqc: e.tensor_copy(lqc[:, :], PF[5][0:128, 0:16]), inc="vv")
    p.free("PF5", "vv")
    p.op(V, lambda e, lqp=lqp: e.tensor_sub(L["tmpa"][:, :], L["laT"][:, :], lqp[:, :]))
    p.op(V, lambda e, lqc=lqc, lqp=lqp: e.tensor_sub(L["tmpb"][:, :], lqc[:, :], lqp[:, :]))
    p.op(V, lambda e, lqc=lqc: e.tensor_sub(L["tmpc"][:, :], lqc[:, :], L["laT"][:, :]),
         inc="vv")
    p.wait(Sc, "vv")
    p.acquire(Sc, "rawv")
    p.op(Sc, lambda e: e.activation(L["rvt"][:, :], L["tmpa"][:, :], AF.Exp))
    p.op(Sc, lambda e: e.activation(L["avt"][:, :], L["tmpb"][:, :], AF.Exp))
    p.op(Sc, lambda e: e.activation(L["wvt"][:, :], L["tmpc"][:, :], AF.Exp), inc="ss")
    p.acquire(Tn, "PB0")
    p.acquire(Tn, "PF0")
    p.op(Tn, lambda e: e.transpose(PB[0][:, 0:128], xbc[:, 16, cc], L["idb"][:, :]))
    p.op(Tn, lambda e: e.matmul(PF[0][:, 0:128], xbc[:, 16, cc], xbc[:, 17, cc],
                                start=True, stop=True), inc="tt")
    p.wait(V, "tt")
    p.acquire(V, "Bpp")
    p.op(V, lambda e: e.tensor_copy(L["Bpp"][:, :], PB[0][:, 0:128]))
    p.op(V, lambda e: e.tensor_tensor(L["Gm"][:, :], PF[0][:, 0:128], L["utb"][:, :],
                                      ALU.mult))
    p.op(V, lambda e: e.tensor_copy(L["ck"][:, t0:t0 + 128], xbc[:, 17, cc]), inc="vv")
    p.free("PB0", "vv")
    p.free("PF0", "vv")
    for g4 in range(4):
        bk = g4 % 2
        p.acquire(Tn, f"PB{bk}")
        for i in range(4):
            p.op(Tn, lambda e, g4=g4, i=i, bk=bk: e.transpose(
                PB[bk][:, 128 * i:128 * i + 128], xbc[:, 4 * g4 + i, cc], L["idb"][:, :]),
                inc="tt" if i == 3 else None)
        p.wait(V, "tt")
        dt_ap = bass.AP(L["dtT"], 4 * g4, [[16, 128], [1, 4], [0, 128]])
        db_ap = bass.AP(L["dbc"], 4 * g4, [[16, 128], [1, 4], [0, 128]])
        p.op(V, lambda e, g4=g4, bk=bk, dt_ap=dt_ap: e.tensor_tensor(
            L["xtl"][:, 4 * g4:4 * g4 + 4, :],
            PB[bk][:, 0:512].rearrange("p (h t) -> p h t", h=4), dt_ap, ALU.mult))
        p.op(V, lambda e, g4=g4, bk=bk, db_ap=db_ap: e.tensor_tensor(
            L["dxh"][:, 4 * g4:4 * g4 + 4, :],
            PB[bk][:, 0:512].rearrange("p (h t) -> p h t", h=4), db_ap, ALU.mult),
            inc="vv")
        p.free(f"PB{bk}", "vv")

    for g4 in range(4):
        ybi = 3 if g4 % 2 == 0 else 5
        ybe = 4 if g4 % 2 == 0 else 0
        p.acquire(Tn, f"PF{ybi}")
        p.acquire(Tn, f"PF{ybe}")
        for hh in range(4):
            h = 4 * g4 + hh
            pbk = 1 if h % 2 else 2
            p.acquire(Tn, f"PF{pbk}")
            p.wait(Tn, "ss")
            p.op(Tn, lambda e, h=h, pbk=pbk: e.matmul(
                PF[pbk][:, 0:128], L["ehs"][:, 128 * h:128 * h + 128],
                lag[0:16, t0:t0 + 128], start=True, stop=True), inc="tt")
            p.acquire(Sc, f"Ps{h % 2}")
            p.wait(Sc, "tt")
            p.op(Sc, lambda e, h=h, pbk=pbk: e.activation(
                L["puf"][:, :], PF[pbk][:, 0:128], AF.Relu,
                bias=L["laT"][:, h:h + 1], scale=-1.0))
            p.op(Sc, lambda e, h=h: e.activation(
                L["Ps"][:, h % 2, :], L["puf"][:, :], AF.Exp, scale=-1.0), inc="ss")
            p.free(f"PF{pbk}", "ss")
            p.acquire(V, f"Ss{h % 2}")
            p.wait(V, "ss")
            p.op(V, lambda e, h=h: e.tensor_tensor(
                L["Ss"][:, h % 2, :], L["Ps"][:, h % 2, :], L["Gm"][:, :],
                ALU.mult), inc="vv")
            p.free(f"Ps{h % 2}", "vv")
            p.wait(Tn, "vv")
            p.op(Tn, lambda e, h=h, hh=hh, ybi=ybi: e.matmul(
                PF[ybi][:, 128 * hh:128 * hh + 128], L["Ss"][:, h % 2, :],
                L["xtl"][:, h, :], start=(hh == 0), stop=(hh == 3)), inc="tt")
            p.free(f"Ss{h % 2}", "tt")
            p.op(Tn, lambda e, h=h, hh=hh, ybe=ybe: e.matmul(
                PF[ybe][:, 128 * hh:128 * hh + 128], xbc[:, 17, cc],
                L["Ht"][:, h, :], start=(hh == 0), stop=(hh == 3)),
                inc="tt" if hh == 3 else None)
        p.wait(V, "tt")
        rv_ap = bass.AP(L["rvt"], 4 * g4, [[16, 128], [1, 4], [0, 128]])
        slot = kg % 2
        p.acquire(V, f"ytl{slot}") if g4 == 0 else None
        p.op(V, lambda e, ybe=ybe, rv_ap=rv_ap: e.tensor_tensor(
            L["scrA"][:, 0:512].rearrange("p (h t) -> p h t", h=4),
            PF[ybe][:, 0:512].rearrange("p (h t) -> p h t", h=4), rv_ap, ALU.mult))
        p.op(V, lambda e, ybi=ybi: e.tensor_add(
            L["scrB"][:, 0:512], PF[ybi][:, 0:512], L["scrA"][:, 0:512]))
        p.op(V, lambda e, g4=g4, slot=slot: e.tensor_tensor(
            L["ytl"][:, slot, 512 * g4:512 * g4 + 512], L["scrB"][:, 0:512],
            L["dxh"][:, 4 * g4:4 * g4 + 4, :].rearrange("p h t -> p (h t)"), ALU.add),
            inc="vv")
        p.free(f"PF{ybi}", "vv")
        p.free(f"PF{ybe}", "vv")
        if g4 == 3:
            p.free("laT", "ss")
    for g4 in range(4):
        bk = 1 if g4 % 2 else 2
        xsc = L["cva"] if g4 % 2 == 0 else L["cvb"]
        wv_ap = bass.AP(L["wvt"], 4 * g4, [[16, 128], [1, 4], [0, 128]])
        p.acquire(V, f"xsc{g4 % 2}")
        p.op(V, lambda e, g4=g4, xsc=xsc, wv_ap=wv_ap: e.tensor_tensor(
            xsc[:, 0:512].rearrange("p (h t) -> p h t", h=4),
            L["xtl"][:, 4 * g4:4 * g4 + 4, :], wv_ap, ALU.mult), inc="vv")
        p.acquire(Tn, f"PF{bk}")
        p.wait(Tn, "vv")
        for i in range(4):
            p.op(Tn, lambda e, g4=g4, i=i, bk=bk, xsc=xsc: e.matmul(
                PF[bk][:, 128 * i:128 * i + 128], L["Bpp"][:, :],
                xsc[:, 128 * i:128 * i + 128], start=(i == 0), stop=(i == 3)),
                inc="tt" if i == 3 else None)
        p.free(f"xsc{g4 % 2}", "tt")
        if g4 == 3:
            p.free("Bpp", "tt")
        p.wait(V, "tt")
        av_ap = bass.AP(L["avt"], 4 * g4, [[16, 128], [1, 4], [0, 128]])
        p.acquire(V, f"Hg{g4}")
        p.op(V, lambda e, g4=g4, av_ap=av_ap: e.tensor_tensor(
            L["scrA"][:, 0:512].rearrange("p (h t) -> p h t", h=4),
            L["H"][:, 4 * g4:4 * g4 + 4, :], av_ap, ALU.mult))
        p.op(V, lambda e, g4=g4, bk=bk: e.tensor_add(
            L["H"][:, 4 * g4:4 * g4 + 4, :].rearrange("p h t -> p (h t)"),
            L["scrA"][:, 0:512], PF[bk][:, 0:512]), inc="vv")
        p.free(f"PF{bk}", "vv")
        p.wait(Sc, "vv")
        p.op(Sc, lambda e, g4=g4: e.copy(
            L["Ht"][:, 4 * g4:4 * g4 + 4, :], L["H"][:, 4 * g4:4 * g4 + 4, :]), inc="ss")
        p.free(f"Hg{g4}", "ss")
        if g4 == 3:
            p.free("rawv", "vv")
    slot = kg % 2
    p.wait("sync", "vv")
    p.op("sync", lambda e, kg=kg, slot=slot: e.dma_start(
        out=L["d_ysc"][kg], in_=L["ytl"][:, slot, :]), inc=f"ly{slot}", n=16)
    p.free(f"ytl{slot}", f"ly{slot}")


def emit_ag(nc, p, L):
    V, Sc, Tn, G = "vector", "scalar", "tensor", "gpsimd"
    p.op("sync", lambda e: e.dma_start(
        out=L["d_agi"][:, 0:2048],
        in_=L["H"][:, :, :].rearrange("p h t -> p (h t)")), inc="lg", n=16)
    p.op("sync", lambda e: e.dma_start(
        out=L["d_agi"][:, 2048:2052], in_=L["lqz"][:, 0:4],
        allow_slow_non_contiguous=True), inc="lg", n=16)
    p.wait("sync", "lg")
    p.op("sync", lambda e: e.dma_start(
        out=L["d_agi"][0:16, 2048:2049], in_=L["lag"][:, 2047:2048],
        allow_slow_non_contiguous=True), inc="lg", n=16)
    p.wait(G, "lg")
    p.op(G, lambda e: e.collective_compute(
        "AllGather", ALU.bypass, replica_groups=[[0, 1, 2, 3], [4, 5, 6, 7]],
        ins=[L["agi_t"].ap().opt()], outs=[L["ago_t"].ap().opt()]), inc="cc")
    p.wait("sync", "cc")
    p.wait(G, "cc")
    for j in range(4):
        p.op("sync", lambda e, j=j: e.dma_start(
            out=L["tcol"][:, j:j + 1], in_=L["d_ago"][128 * j:128 * j + 16, 2048:2049],
            allow_slow_non_contiguous=True), inc="lg", n=16)
    p.wait(V, "lg")
    p.op(V, lambda e: e.tensor_tensor_scan(
        L["theta"][:, :], L["o16w"][:, 0:4], L["tcol"][:, :], 0.0, ALU.mult, ALU.add))
    p.op(V, lambda e: e.tensor_mul(L["tcol"][:, :], L["theta"][:, :], L["ohot"][:, :]))
    p.op(V, lambda e: e.reduce_sum(L["dtbe"][:, 0:1] if False else L["tmpa"][0:16, 0:1],
                                   L["tcol"][:, :], AXX))
    p.op(V, lambda e: e.tensor_scalar(
        L["theta"][:, :], L["theta"][:, :], L["tmpa"][0:16, 0:1], 0.0,
        ALU.subtract, ALU.max), inc="vv")
    p.wait(Sc, "vv")
    p.op(Sc, lambda e: e.activation(L["theta"][:, :], L["theta"][:, :], AF.Exp, scale=-1.0),
         inc="ss")
    p.wait(V, "ss")
    p.op(V, lambda e: e.tensor_mul(L["theta"][:, :], L["theta"][:, :], L["mmask"][:, :]),
         inc="vv")
    for j in range(4):
        p.acquire(G, "gtile")
        p.op(G, lambda e, j=j: e.dma_start(
            out=L["ytl"][:, 2, :], in_=L["d_ago"][128 * j:128 * j + 128, 0:2048]),
            inc="lgg", n=16)
        p.op(V, lambda e, j=j: e.tensor_scalar(
            L["tmpb"][0:16, :], L["id16"][:, :], L["theta"][:, j:j + 1], None, ALU.mult),
            inc="vv")
        p.acquire(Tn, "PF5")
        p.wait(Tn, "vv")
        p.op(Tn, lambda e: e.matmul(L["PF"][5][0:128, 0:16], L["o16"][:, :],
                                    L["tmpb"][0:16, 0:16], start=True, stop=True), inc="tt")
        p.wait(V, "tt")
        p.op(V, lambda e: e.tensor_copy(L["thv"][:, :], L["PF"][5][0:128, 0:16]), inc="vv")
        p.free("PF5", "vv")
        p.wait(V, "lg")
        p.wait(V, "lgg")
        th_ap = bass.AP(L["thv"], 0, [[16, 128], [1, 16], [0, 128]])
        p.op(V, lambda e, th_ap=th_ap: e.tensor_tensor(
            L["xtl"][:, :, :], L["ytl"][:, 2, :].rearrange("p (h t) -> p h t", h=16),
            th_ap, ALU.mult), inc="vv")
        p.free("gtile", "vv")
        p.op(V, lambda e: e.tensor_add(
            L["sinb"][:, :, :].rearrange("p h t -> p (h t)"),
            L["sinb"][:, :, :].rearrange("p h t -> p (h t)"),
            L["xtl"][:, :, :].rearrange("p h t -> p (h t)")), inc="vv")


def emit_corr(nc, p, L, kg):
    PF = L["PF"]
    V, Sc, Tn = "vector", "scalar", "tensor"
    t0 = 128 * kg
    slot = 2 + (kg % 2)
    p.acquire("sync", f"ytl{slot}")
    p.wait("sync", f"ly{kg % 2}")
    p.op("sync", lambda e, kg=kg, slot=slot: e.dma_start(
        out=L["ytl"][:, slot, :], in_=L["d_ysc"][kg]), inc=f"ly{slot}", n=16)
    p.acquire(Tn, "PF4")
    p.op(Tn, lambda e: e.transpose(PF[4][:, 0:16], L["lag"][:, t0:t0 + 128],
                                   L["id16"][:, :]), inc="tt")
    p.wait(V, "tt")
    p.op(V, lambda e: e.tensor_copy(L["laT"][:, :], PF[4][:, 0:16]), inc="vv")
    p.free("PF4", "vv")
    p.wait(Sc, "vv")
    p.op(Sc, lambda e: e.activation(L["rvt"][:, :], L["laT"][:, :], AF.Exp), inc="ss")
    for g4 in range(4):
        p.acquire(Tn, f"PF{g4}")
        p.op(Tn, lambda e, g4=g4: e.matmul(
            PF[g4][:, 0:512], L["ck"][:, t0:t0 + 128],
            L["sinb"][:, 4 * g4:4 * g4 + 4, :].rearrange("p h t -> p (h t)"),
            start=True, stop=True), inc="tt")
    p.wait(V, "tt")
    p.wait(V, "ss")
    p.wait(V, f"ly{slot}")
    for g4 in range(4):
        rv_ap = bass.AP(L["rvt"], 4 * g4, [[16, 128], [1, 4], [0, 128]])
        p.op(V, lambda e, g4=g4, rv_ap=rv_ap: e.tensor_tensor(
            L["scrA"][:, 0:512].rearrange("p (h t) -> p h t", h=4),
            PF[g4][:, 0:512].rearrange("p (h t) -> p h t", h=4), rv_ap, ALU.mult))
        p.op(V, lambda e, g4=g4, slot=slot: e.tensor_add(
            L["ytl"][:, slot, 512 * g4:512 * g4 + 512],
            L["ytl"][:, slot, 512 * g4:512 * g4 + 512], L["scrA"][:, 0:512]),
            inc="vv")
        p.free(f"PF{g4}", "vv")
    p.wait("sync", "vv")
    p.op("sync", lambda e, kg=kg, slot=slot: e.dma_start(
        out=L["d_ysc"][kg], in_=L["ytl"][:, slot, :]), inc=f"ly{slot}", n=16)
    p.free(f"ytl{slot}", f"ly{slot}")


def emit_tail(nc, p, L, tq):
    PF, PB = L["PF"], L["PB"]
    xnb, wsl = L["xnb"], L["wsl"]
    V, Sc, Tn, G = "vector", "scalar", "tensor", "gpsimd"
    p.wait("sync", "ly2")
    p.wait("sync", "ly3")
    for i in range(4):
        p.acquire("sync", f"ytl{i}")
        p.op("sync", lambda e, i=i, tq=tq: e.dma_start(
            out=L["ytl"][:, i, :], in_=L["d_ysc"][4 * tq + i]), inc=f"ly{i}", n=16)
        p.wait(V, f"ly{i}")
    for ofc in range(8):
        buf = ofc % 2
        p.acquire(G, f"wsl{buf}")
        for k in range(8):
            p.op(G, lambda e, k=k, ofc=ofc, buf=buf: e.dma_start(
                out=wsl[:, buf, k, 0:256],
                in_=L["d_win"][128 * k:128 * (k + 1), 256 * ofc:256 * ofc + 256]),
                inc=f"lw{buf}", n=16)
        p.wait(Tn, f"lw{buf}")
        p.acquire(Tn, "PF4")
        for k in range(8):
            p.op(Tn, lambda e, k=k, buf=buf: e.matmul(
                PF[4][0:1, 0:256], xnb[:, k, 2051:2052], wsl[:, buf, k, 0:256],
                start=(k == 0), stop=(k == 7)), inc="tt" if k == 7 else None)
        p.wait(V, "tt")
        p.acquire(V, "zrow")
        p.op(V, lambda e: e.tensor_copy(L["zrow"][0:1, 0:256], PF[4][0:1, 0:256]), inc="vv")
        p.free("PF4", "vv")
        p.wait(Tn, "vv")
        for ti in range(4):
            bk = ti % 2
            tc0 = 512 * tq + 128 * ti + 3
            p.acquire(Tn, f"PF{bk}")
            for k in range(8):
                p.op(Tn, lambda e, k=k, buf=buf, bk=bk, tc0=tc0: e.matmul(
                    PF[bk][:, 0:256], xnb[:, k, tc0:tc0 + 128], wsl[:, buf, k, 0:256],
                    start=(k == 0), stop=False))
            p.op(Tn, lambda e, bk=bk: e.matmul(
                PF[bk][:, 0:256], L["o1b"][:, :], L["zrow"][0:1, 0:256],
                start=False, stop=True), inc="tt")
            if ti == 3:
                p.free("zrow", "tt")
            p.wait(Sc, "tt")
            gsl = ti % 2
            p.acquire(Sc, f"gz{gsl}")
            p.op(Sc, lambda e, bk=bk, gsl=gsl: e.activation(
                L["gzt"][:, gsl, 0:256], PF[bk][:, 0:256], AF.Silu), inc="ss")
            p.free(f"PF{bk}", "ss")
            p.wait(V, "ss")
            p.op(V, lambda e, ti=ti, ofc=ofc, gsl=gsl: e.tensor_tensor(
                L["ytl"][:, ti, 256 * ofc:256 * ofc + 256],
                L["ytl"][:, ti, 256 * ofc:256 * ofc + 256], L["gzt"][:, gsl, 0:256],
                ALU.mult))
            p.op(V, lambda e, ti=ti, ofc=ofc, gsl=gsl: e.scalar_tensor_tensor(
                L["scrQ"][:, 0:256], L["ytl"][:, ti, 256 * ofc:256 * ofc + 256], 1.0,
                L["ytl"][:, ti, 256 * ofc:256 * ofc + 256], ALU.bypass, ALU.mult,
                accum_out=L["rsums"][:, 8 * ti + ofc:8 * ti + ofc + 1]), inc="vv")
            p.free(f"gz{gsl}", "vv")
        p.free(f"wsl{buf}", "tt")
    for ti in range(4):
        p.op(V, lambda e, ti=ti: e.reduce_sum(
            L["qcolv"][:, :], L["rsums"][:, 8 * ti:8 * ti + 8], AXX))
        p.op(V, lambda e: e.tensor_scalar_mul(L["qcolv"][:, :], L["qcolv"][:, :], 1.0 / 2048),
             inc="vv")
        p.wait(Sc, "vv")
        p.op(Sc, lambda e: e.activation(L["qcolv"][:, :], L["qcolv"][:, :], AF.Ln, bias=L["cflt"][:, 0:1]))
        p.op(Sc, lambda e: e.activation(L["qcolv"][:, :], L["qcolv"][:, :], AF.Exp, scale=-0.5),
             inc="ss")
        p.wait(V, "ss")
        p.op(V, lambda e, ti=ti: e.tensor_scalar_mul(
            L["ytl"][:, ti, :], L["ytl"][:, ti, :], L["qcolv"][:, :]), inc="vv")
    for ti in range(4):
        for cg in range(4):
            bk = cg % 2
            p.acquire(Tn, f"PB{bk}")
            p.wait(Tn, "vv")
            for i in range(4):
                ct = 4 * cg + i
                p.op(Tn, lambda e, ti=ti, ct=ct, i=i, bk=bk: e.transpose(
                    PB[bk][:, 128 * i:128 * i + 128],
                    L["ytl"][:, ti, 128 * ct:128 * ct + 128], L["idb"][:, :]),
                    inc="tt" if i == 3 else None)
            p.wait(V, "tt")
            p.op(V, lambda e, ti=ti, cg=cg, bk=bk: e.tensor_copy(
                L["yfm"][:, 4 * cg:4 * cg + 4, 128 * ti:128 * ti + 128],
                PB[bk][:, 0:512].rearrange("p (c t) -> p c t", c=4)), inc="vv")
            p.free(f"PB{bk}", "vv")
    for sl_, dstn in ((0, "abct"), (1, "bbct")):
        p.acquire(Tn, "PF4")
        for i in range(4):
            p.acquire(V, "dgs")
            p.op(V, lambda e, i=i, tq=tq, sl_=sl_: e.tensor_scalar(
                L["dgs"][:, :], L["idf"][:, :], L["pqa"][:, sl_, 4 * tq + i:4 * tq + i + 1],
                None, ALU.mult), inc="vv")
            p.wait(Tn, "vv")
            p.op(Tn, lambda e, i=i: e.matmul(PF[4][:, 128 * i:128 * i + 128],
                                             L["o128"][:, :], L["dgs"][:, :],
                                             start=True, stop=True), inc="tt")
            p.free("dgs", "tt")
        p.wait(V, "tt")
        p.op(V, lambda e, dstn=dstn: e.tensor_copy(L[dstn][:, :], PF[4][:, 0:512]), inc="vv")
        p.free("PF4", "vv")
    p.acquire(Tn, "PF4")
    p.acquire(Tn, "PF5")
    for pas in range(2):
        for o4 in range(4):
            p.acquire(Tn, f"PF{o4}")
        for k in range(16):
            wb = k % 2
            p.acquire(G, f"wosb{wb}")
            p.op(G, lambda e, k=k, pas=pas, wb=wb: e.dma_start(
                out=L["wosb"][:, wb, :],
                in_=L["d_wout"][128 * k:128 * (k + 1), 512 * pas:512 * pas + 512]),
                inc=f"lo{wb}", n=16)
            p.wait(V, f"lo{wb}")
            p.op(V, lambda e, k=k, wb=wb: e.tensor_scalar(
                L["wosb"][:, wb, :], L["wosb"][:, wb, :], L["rmsw"][:, k:k + 1],
                None, ALU.mult), inc="vv")
            p.wait(Tn, "vv")
            for o4 in range(4):
                p.op(Tn, lambda e, k=k, o4=o4, wb=wb: e.matmul(
                    PF[o4][:, 0:512], L["wosb"][:, wb, 128 * o4:128 * o4 + 128],
                    L["yfm"][:, k, :], start=(k == 0), stop=(k == 15)),
                    inc="tt" if o4 == 3 else None)
            p.free(f"wosb{wb}", "tt")
        p.wait(V, "tt")
        for o4 in range(4):
            o = 4 * pas + o4
            p.acquire("sync", "xo1")
            p.op("sync", lambda e, o=o, tq=tq: e.dma_start(
                out=L["xo1"][:, :],
                in_=L["d_xf"][128 * o:128 * (o + 1), 512 * tq + 3:512 * tq + 515]),
                inc="lx", n=16)
            p.wait(V, "lx")
            xslot = o4 % 2
            p.acquire(V, f"xnw{xslot}")
            p.op(V, lambda e: e.tensor_mul(L["scrA"][:, 0:512], L["xo1"][:, :],
                                           L["abct"][:, :]))
            p.op(V, lambda e, o4=o4: e.tensor_add(L["scrB"][:, 0:512],
                                                  L["scrA"][:, 0:512], PF[o4][:, 0:512]))
            p.op(V, lambda e, xslot=xslot: e.tensor_sub(
                L["xnw"][:, xslot, :], L["scrB"][:, 0:512], L["bbct"][:, :]), inc="vv")
            p.free("xo1", "vv")
            p.free(f"PF{o4}", "vv")
            p.wait(V, "ss")
            p.wait(V, "tt")
            p.op(V, lambda e, xslot=xslot: e.tensor_copy(L["xbt"][:, :], L["xnw"][:, xslot, :]),
                 inc="vv")
            p.wait("sync", "vv")
            p.op("sync", lambda e, o=o, tq=tq, xslot=xslot: e.dma_start(
                out=L["d_xns"][o, :, 512 * tq:512 * tq + 512], in_=L["xnw"][:, xslot, :]),
                inc=f"lz{xslot}", n=16)
            p.free(f"xnw{xslot}", f"lz{xslot}")
            p.wait(Sc, "vv")
            p.wait(Sc, "tt")
            p.op(Sc, lambda e: e.activation(L["xsq2"][:, :], L["xbt"][:, :], AF.Square),
                 inc="ss")
            p.wait(Tn, "vv")
            p.wait(Tn, "ss")
            p.op(Tn, lambda e, o=o: e.matmul(
                PF[5][0:1, 0:512], L["ocb"][:, :], L["xbt"][:, :],
                start=(o == 0), stop=(o == 7)), inc="tt")
            p.op(Tn, lambda e, o=o: e.matmul(
                PF[4][0:1, 0:512], L["ocb"][:, :], L["xsq2"][:, :],
                start=(o == 0), stop=(o == 7)),
                inc="tt")
    p.wait(V, "tt")
    rS = lambda i: L["rows"][0:1, i, 0:512]
    p.op(V, lambda e: e.tensor_scalar(rS(2), PF[5][0:1, 0:512], 1.0 / 1024, None, ALU.mult))
    p.op(V, lambda e: e.tensor_scalar(rS(3), PF[4][0:1, 0:512], 1.0 / 1024, None, ALU.mult),
         inc="vv")
    p.free("PF5", "vv")
    p.free("PF4", "vv")
    p.op(V, lambda e: e.tensor_mul(L["rowz"][0:1, :], rS(2), rS(2)))
    p.op(V, lambda e: e.tensor_sub(rS(3), rS(3), L["rowz"][0:1, :]), inc="vv")
    p.wait(Sc, "vv")
    p.op(Sc, lambda e: e.activation(rS(3), rS(3), AF.Ln, bias=L["cflt"][0:1, 0:1]))
    p.op(Sc, lambda e: e.activation(rS(3), rS(3), AF.Exp, scale=-0.5), inc="ss")
    p.wait(Tn, "ss")
    p.acquire(Tn, "PF4")
    for sl_, dc in ((2, 0), (3, 1)):
        for i in range(4):
            p.op(Tn, lambda e, sl_=sl_, i=i: e.transpose(
                PF[4][:, 4 * (sl_ - 2) + i:4 * (sl_ - 2) + i + 1],
                L["rows"][0:1, sl_, 128 * i:128 * i + 128], L["id1"][:, :]),
                inc="tt" if (sl_, i) == (3, 3) else None)
    p.wait(V, "tt")
    p.op(V, lambda e, tq=tq: e.tensor_copy(L["pq2"][:, 0, 4 * tq:4 * tq + 4], PF[4][:, 0:4]))
    p.op(V, lambda e, tq=tq: e.tensor_copy(L["pq2"][:, 1, 4 * tq:4 * tq + 4], PF[4][:, 4:8]),
         inc="vv")
    p.free("PF4", "vv")
    if "dumpyfm" in DBG_FLAGS:
        p.wait("sync", "vv")
        p.op("sync", lambda e, tq=tq: e.dma_start(
            out=L["d_yfm"][tq], in_=L["yfm"][:, :, :]), inc="ou", n=16)


def emit_ffn(nc, p, L):
    PF = L["PF"]
    V, Sc, Tn, G = "vector", "scalar", "tensor", "gpsimd"

    def load_lora(j):
        p.acquire(G, "lab1")
        p.op(G, lambda e, j=j: e.dma_start(out=L["lab1"][:, :], in_=L["d_la"][j]),
             inc="lqg", n=16)
        p.op(G, lambda e, j=j: e.dma_start(out=L["lbb1"][:, :], in_=L["d_lb"][j]),
             inc="lqg", n=16)

    def dequant(j):
        tgt = j % 2
        load_lora(j)
        p.acquire(V, f"weff{tgt}")
        for k in range(8):
            p.acquire("sync", "qsl")
            p.op("sync", lambda e, j=j, k=k: e.dma_start(
                out=L["qsl"][:, :], in_=L["d_qt"][j, 128 * k:128 * (k + 1), :]),
                inc="lq", n=16)
            p.acquire(G, "ssl")
            p.op(G, lambda e, j=j, k=k: e.dma_start(
                out=L["ssl"][:, :], in_=L["d_st"][j, 128 * k:128 * (k + 1), :]),
                inc="lqg", n=16)
            p.acquire(Sc, "t1s")
            p.wait(Sc, "lq")
            p.op(Sc, lambda e: e.activation(
                L["t1s"][:, :], L["qsl"][:, :], AF.Identity, bias=L["cflt"][:, 1:2], scale=1.0 / 7.5),
                inc="ss")
            p.free("qsl", "ss")
            p.wait(V, "ss")
            p.wait(V, "lqg")
            p.op(V, lambda e, k=k, tgt=tgt: e.tensor_tensor(
                L["weff"][:, tgt, k, :], L["t1s"][:, :], L["ssl"][:, :], ALU.mult), inc="vv")
            p.free("ssl", "vv")
            p.free("t1s", "vv")
            for half in range(2):
                p.acquire(Tn, "PF5")
                p.wait(Tn, "lqg")
                p.op(Tn, lambda e, j=j, k=k, half=half: e.matmul(
                    PF[5][:, 0:512], L["lab1"][:, 128 * k:128 * (k + 1)],
                    L["lbb1"][:, 512 * half:512 * half + 512], start=True, stop=True),
                    inc="tt")
                p.acquire(Sc, "lsc")
                p.wait(Sc, "tt")
                p.op(Sc, lambda e: e.copy(L["lsc"][:, :], PF[5][:, 0:512]), inc="ss")
                p.free("PF5", "ss")
                p.wait(V, "ss")
                p.op(V, lambda e, k=k, tgt=tgt, half=half: e.tensor_add(
                    L["weff"][:, tgt, k, 512 * half:512 * half + 512],
                    L["weff"][:, tgt, k, 512 * half:512 * half + 512],
                    L["lsc"][:, :]), inc="vv")
                p.free("lsc", "vv")
        p.bankfree[f"wrdy{tgt}"] = ("vv", p.cnt["vv"])
        p.free("lab1", "tt")

    def wait_weight(eng, tgt):
        if f"wrdy{tgt}" in p.bankfree:
            sn, v = p.bankfree[f"wrdy{tgt}"]
            p.wait(eng, sn, v)

    p.wait("sync", "lz0")
    p.wait("sync", "lz1")
    for ch in range(4):
        p.acquire("sync", "xnt2")
        p.op("sync", lambda e, ch=ch: e.dma_start(
            out=L["xnt2"][:, :, :],
            in_=L["d_xns"].rearrange("o p t -> p o t")[:, :, 512 * ch:512 * ch + 512]),
            inc="lzl", n=16)
        for sl_, dst in ((0, "mbc"), (1, "rbc")):
            p.acquire(Tn, "PF4")
            for i in range(4):
                p.acquire(V, "dgs")
                p.op(V, lambda e, sl_=sl_, ch=ch, i=i: e.tensor_scalar(
                    L["dgs"][:, :], L["idf"][:, :],
                    L["pq2"][:, sl_, 4 * ch + i:4 * ch + i + 1], None, ALU.mult), inc="vv")
                p.wait(Tn, "vv")
                p.op(Tn, lambda e, i=i: e.matmul(
                    PF[4][:, 128 * i:128 * i + 128], L["o128"][:, :], L["dgs"][:, :],
                    start=True, stop=True), inc="tt")
                p.free("dgs", "tt")
            p.wait(V, "tt")
            p.op(V, lambda e, dst=dst: e.tensor_copy(L[dst][:, :], PF[4][:, 0:512]),
                 inc="vv")
            p.free("PF4", "vv")
        p.wait(V, "lzl")
        for k in range(8):
            p.op(V, lambda e, k=k: e.tensor_sub(
                L["scrA"][:, 0:512], L["xnt2"][:, k, :], L["mbc"][:, :]))
            p.op(V, lambda e, k=k, ch=ch: e.tensor_tensor(
                L["hbuf"][:, k, 512 * ch:512 * ch + 512], L["scrA"][:, 0:512],
                L["rbc"][:, :], ALU.mult), inc="vv" if k == 7 else None)
        p.free("xnt2", "vv")

    dequant(0)
    dequant(1)
    for b in range(3):
        j1, j2 = 2 * b, 2 * b + 1
        wait_weight(Tn, 0)
        for o in range(8):
            for ch in range(4):
                p.acquire(Tn, f"PF{ch}")
            for k in range(8):
                for ch in range(4):
                    p.op(Tn, lambda e, o=o, k=k, ch=ch: e.matmul(
                        PF[ch][:, 0:512], L["weff"][:, 0, k, 128 * o:128 * o + 128],
                        L["hbuf"][:, k, 512 * ch:512 * ch + 512],
                        start=(k == 0), stop=(k == 7)),
                        inc="tt" if (k == 7 and ch == 3) else None)
            p.wait(V, "tt")
            for ch in range(4):
                p.op(V, lambda e, o=o, ch=ch, j1=j1: e.tensor_scalar_add(
                    L["y1"][:, o, 512 * ch:512 * ch + 512], PF[ch][:, 0:512],
                    L["fbias"][:, 8 * j1 + o:8 * j1 + o + 1]), inc="vv")
                p.free(f"PF{ch}", "vv")
        p.free("weff0", "tt")
        for ch in range(4):
            p.acquire(Tn, "PF4")
            p.acquire(Tn, "PF5")
            for o in range(8):
                p.wait(Sc, "vv")
                p.wait(Sc, "tt")
                p.op(Sc, lambda e, o=o, ch=ch: e.activation(
                    L["sqs"][:, :], L["y1"][:, o, 512 * ch:512 * ch + 512], AF.Square),
                    inc="ss")
                p.wait(Tn, "ss")
                p.op(Tn, lambda e, o=o, ch=ch: e.matmul(
                    PF[4][0:1, 0:512], L["ocb"][:, :],
                    L["y1"][:, o, 512 * ch:512 * ch + 512],
                    start=(o == 0), stop=(o == 7)))
                p.op(Tn, lambda e, o=o: e.matmul(
                    PF[5][0:1, 0:512], L["ocb"][:, :], L["sqs"][:, :],
                    start=(o == 0), stop=(o == 7)), inc="tt")
            p.wait(V, "tt")
            rS = lambda i: L["rows"][0:1, i, 0:512]
            p.op(V, lambda e: e.tensor_scalar(rS(2), PF[4][0:1, 0:512], 1.0 / 1024, None,
                                              ALU.mult))
            p.op(V, lambda e: e.tensor_scalar(rS(3), PF[5][0:1, 0:512], 1.0 / 1024, None,
                                              ALU.mult), inc="vv")
            p.free("PF4", "vv")
            p.free("PF5", "vv")
            p.op(V, lambda e: e.tensor_mul(L["rowz"][0:1, :], rS(2), rS(2)))
            p.op(V, lambda e: e.tensor_sub(rS(3), rS(3), L["rowz"][0:1, :]), inc="vv")
            p.wait(Sc, "vv")
            p.op(Sc, lambda e: e.activation(rS(3), rS(3), AF.Ln, bias=L["cflt"][0:1, 0:1]))
            p.op(Sc, lambda e: e.activation(rS(3), rS(3), AF.Exp, scale=-0.5), inc="ss")
            p.acquire(Tn, "PF4")
            p.acquire(Tn, "PF5")
            p.wait(Tn, "ss")
            p.op(Tn, lambda e: e.matmul(PF[4][:, 0:512], L["o1f"][:, :], rS(2),
                                        start=True, stop=True))
            p.op(Tn, lambda e: e.matmul(PF[5][:, 0:512], L["o1f"][:, :], rS(3),
                                        start=True, stop=True), inc="tt")
            p.wait(V, "tt")
            for o in range(8):
                p.wait(V, "ss")
                p.op(V, lambda e, o=o, ch=ch: e.tensor_sub(
                    L["scrA"][:, 0:512], L["y1"][:, o, 512 * ch:512 * ch + 512],
                    PF[4][:, 0:512]))
                p.op(V, lambda e, o=o: e.tensor_tensor(
                    L["scrB"][:, 0:512], L["scrA"][:, 0:512], PF[5][:, 0:512], ALU.mult),
                    inc="vv")
                p.wait(Sc, "vv")
                p.op(Sc, lambda e, o=o, ch=ch, b=b: e.activation(
                    L["y1"][:, o, 512 * ch:512 * ch + 512], L["scrB"][:, 0:512], AF.Gelu,
                    scale=L["fgam"][:, 8 * b + o:8 * b + o + 1],
                    bias=L["fbet"][:, 8 * b + o:8 * b + o + 1]), inc="ss")
            p.free("PF4", "vv")
            p.free("PF5", "vv")
        wait_weight(Tn, 1)
        p.wait(Tn, "ss")
        for o in range(8):
            for ch in range(4):
                p.acquire(Tn, f"PF{ch}")
            for k in range(8):
                for ch in range(4):
                    p.op(Tn, lambda e, o=o, k=k, ch=ch: e.matmul(
                        PF[ch][:, 0:512], L["weff"][:, 1, k, 128 * o:128 * o + 128],
                        L["y1"][:, k, 512 * ch:512 * ch + 512],
                        start=(k == 0), stop=(k == 7)),
                        inc="tt" if (k == 7 and ch == 3) else None)
            p.wait(V, "tt")
            for ch in range(4):
                p.op(V, lambda e, o=o, ch=ch, j2=j2: e.scalar_tensor_tensor(
                    L["hbuf"][:, o, 512 * ch:512 * ch + 512], PF[ch][:, 0:512],
                    L["fbias"][:, 8 * j2 + o:8 * j2 + o + 1],
                    L["hbuf"][:, o, 512 * ch:512 * ch + 512], ALU.add, ALU.add),
                    inc="vv")
                p.free(f"PF{ch}", "vv")
        p.free("weff1", "tt")
        if b < 2:
            dequant(2 * b + 2)
            dequant(2 * b + 3)
    for ch in range(4):
        p.acquire("sync", "xnt2")
        p.op("sync", lambda e, ch=ch: e.dma_start(
            out=L["xnt2"][:, :, :],
            in_=L["d_xns"].rearrange("o p t -> p o t")[:, :, 512 * ch:512 * ch + 512]),
            inc="lzl", n=16)
        p.wait(V, "lzl")
        for k in range(8):
            oslot = k % 2
            p.acquire(V, f"outt{oslot}")
            p.op(V, lambda e, k=k, ch=ch, oslot=oslot: e.tensor_add(
                L["outt"][:, oslot, :], L["xnt2"][:, k, :],
                L["hbuf"][:, k, 512 * ch:512 * ch + 512]), inc="vv")
            p.wait("sync", "vv")
            p.op("sync", lambda e, k=k, ch=ch, oslot=oslot: e.dma_start(
                out=L["d_out"][k, :, 512 * ch:512 * ch + 512], in_=L["outt"][:, oslot, :]),
                inc=f"ob{oslot}", n=16)
            p.free(f"outt{oslot}", f"ob{oslot}")
        p.free("xnt2", "vv")


def _stage(inputs):
    x = np.asarray(inputs["L0_attn_x"], np.float32)
    xf = np.ascontiguousarray(x.transpose(0, 2, 1))
    shared = {
        "w_in": np.ascontiguousarray(np.asarray(inputs["L0_attn_W_in"], np.float32)),
        "w_out": np.ascontiguousarray(np.asarray(inputs["L0_attn_W_out"], np.float32)),
        "conv_w": np.ascontiguousarray(
            np.asarray(inputs["L0_attn_conv_w"], np.float32).reshape(18, 128, 4)
            .transpose(1, 0, 2).reshape(128, 72)),
        "conv_b": np.ascontiguousarray(
            np.asarray(inputs["L0_attn_conv_b"], np.float32).reshape(18, 128).T),
        "ln_w": np.ascontiguousarray(
            np.asarray(inputs["L0_attn_ln_w"], np.float32).reshape(8, 128).T),
        "ln_b": np.ascontiguousarray(
            np.asarray(inputs["L0_attn_ln_b"], np.float32).reshape(8, 128).T),
        "dt_bias": np.ascontiguousarray(
            np.asarray(inputs["L0_attn_dt_bias"], np.float32).reshape(16, 1)),
        "a_log": np.ascontiguousarray(
            np.asarray(inputs["L0_attn_A_log"], np.float32).reshape(16, 1)),
        "d_vec": np.ascontiguousarray(
            np.asarray(inputs["L0_attn_D"], np.float32).reshape(16, 1)),
        "rms_w": np.ascontiguousarray(
            np.asarray(inputs["L0_attn_rms_w"], np.float32).reshape(16, 128).T),
        "q_t": np.ascontiguousarray(
            np.asarray(inputs["L0_ffn_qweights"], np.int32).transpose(0, 2, 1)),
        "s_t": np.ascontiguousarray(np.repeat(
            np.asarray(inputs["L0_ffn_scales"], np.float32).transpose(0, 2, 1), 16, axis=1)),
        "f_bias": np.ascontiguousarray(
            np.asarray(inputs["L0_ffn_biases"], np.float32).reshape(6, 8, 128)
            .transpose(2, 0, 1).reshape(128, 48)),
        "lora_a": np.ascontiguousarray(np.asarray(inputs["L0_ffn_lora_a"], np.float32)),
        "lora_bt": np.ascontiguousarray(
            np.asarray(inputs["L0_ffn_lora_b"], np.float32).transpose(0, 2, 1)),
        "f_gam": np.ascontiguousarray(
            np.asarray(inputs["L0_ffn_ln_gamma"], np.float32).reshape(3, 8, 128)
            .transpose(2, 0, 1).reshape(128, 24)),
        "f_bet": np.ascontiguousarray(
            np.asarray(inputs["L0_ffn_ln_beta"], np.float32).reshape(3, 8, 128)
            .transpose(2, 0, 1).reshape(128, 24)),
    }
    in_maps = []
    for c in range(8):
        b, r = c // 4, c % 4
        t0 = r * 2048
        xfc = np.zeros((1024, 2051), np.float32)
        xfc[:, 3:] = xf[b][:, t0:t0 + 2048]
        if r > 0:
            xfc[:, 0:3] = xf[b][:, t0 - 3:t0]
        mm_ = np.zeros((16, 4), np.float32)
        mm_[:, :r] = 1.0
        oh_ = np.zeros((16, 4), np.float32)
        oh_[:, max(r - 1, 0)] = 1.0
        pm_ = np.full((128, 4), 0.0 if r == 0 else 1.0, np.float32)
        m = dict(shared)
        m.update({"xf": xfc, "m_mask": mm_, "o_hot": oh_, "p_mask": pm_})
        in_maps.append(m)
    return x, in_maps


def kernel(**inputs):
    x, in_maps = _stage(inputs)
    nc = build(None)
    res = run_bass_kernel_spmd(nc, in_maps, core_ids=list(range(8)))
    out = np.empty((2, 8192, 1024), np.float32)
    for c in range(8):
        b, r = c // 4, c % 4
        of = np.asarray(res.results[c]["out"]).reshape(1024, 2048)
        out[b, r * 2048:(r + 1) * 2048, :] = of.T
    return out
